# version 17
# baseline (speedup 1.0000x reference)
import numpy as np
import ml_dtypes
from contextlib import ExitStack

import concourse.bass as bass
import concourse.mybir as mybir
from concourse.bass_utils import run_bass_kernel_spmd

F32 = mybir.dt.float32
BF16 = mybir.dt.bfloat16
AF = mybir.ActivationFunctionType
ALU = mybir.AluOpType
AX = mybir.AxisListType

MASK_NEG = -1.0e9
EPS = 1e-5
TH = 256

FULL = dict(D=4096, TC=1024, DH=256, HPC=2, E=8, stages=3)


class Ctx:
    def __init__(self, nc):
        self.nc = nc
        self.q = {k: [] for k in "TVSGY"}
        self.cnt = {k: 0 for k in "TVSGY"}
        self.sem = {}
        self.dcnt = {}
        self.waited = {k: {} for k in "TVSGY"}

    def mksem(self, stack, name):
        self.sem[name] = stack.enter_context(self.nc.semaphore(name))
        self.dcnt[name] = 0

    def _waits(self, e, waits):
        real = []
        for s, v in waits:
            if v and v > 0 and self.waited[e].get(s, -1) < v:
                real.append((s, v))
                self.waited[e][s] = v
        return real

    def op(self, e, fn, waits=(), tick=True):
        real = self._waits(e, waits)
        myc = None
        if tick:
            self.cnt[e] += 1
            myc = self.cnt[e]
        esem = "e" + e

        def emit(eng, real=real, fn=fn, tick=tick, esem=esem):
            for s, v in real:
                eng.wait_ge(self.sem[s], v)
            inst = fn(eng)
            if tick and inst is not None:
                inst.then_inc(self.sem[esem], 1)
        self.q[e].append(emit)
        return myc

    def dma(self, e, dsem, fn, waits=(), n=1):
        real = self._waits(e, waits)
        self.dcnt[dsem] += 16 * n
        tgt = self.dcnt[dsem]

        def emit(eng, real=real, fn=fn, dsem=dsem):
            for s, v in real:
                eng.wait_ge(self.sem[s], v)
            insts = fn(eng)
            if not isinstance(insts, (list, tuple)):
                insts = [insts]
            for i in insts:
                i.then_inc(self.sem[dsem], 16)
        self.q[e].append(emit)
        return tgt

    def cc(self, ins_ap, outs_ap, groups, waits=()):
        real = self._waits("G", waits)
        self.dcnt["cc"] += 1
        tgt = self.dcnt["cc"]

        def emit(eng, real=real):
            for s, v in real:
                eng.wait_ge(self.sem[s], v)
            eng.collective_compute("AllGather", ALU.bypass, replica_groups=groups,
                                   ins=[ins_ap], outs=[outs_ap]).then_inc(self.sem["cc"], 1)
        self.q["G"].append(emit)
        return tgt

    def barrier(self, engines="TVSGY", extra=()):
        marks = [("e" + e2, self.cnt[e2]) for e2 in "TVS"]
        marks += [(s, v) for s, v in self.dcnt.items() if v > 0 and s != "cc"]
        marks += list(extra)
        for e in engines:
            self.op(e, lambda eng: eng.nop(), waits=marks, tick=False)

    def flush(self, block):
        for e, attr in (("G", "gpsimd"), ("Y", "sync"), ("T", "tensor"),
                        ("V", "vector"), ("S", "scalar")):
            lst = self.q[e]
            if lst:
                self.q[e] = []
                getattr(block, attr)(lambda eng, lst=lst: [f(eng) for f in lst])


def _act_table_idx(nc):
    from concourse.hw_specs import get_activation_tables
    for idx, name in enumerate(get_activation_tables(nc.m.arch)):
        if name == "natural_log_exp_and_others":
            return idx
    raise ValueError("act table set not found")


def build(cfg):
    D, TC, DH, HPC, E = cfg["D"], cfg["TC"], cfg["DH"], cfg["HPC"], cfg["E"]
    stages = cfg["stages"]
    DC = D // 128
    SEQ = 4 * TC
    NTB = SEQ // 512
    NQ = TC // TH
    NT = TH // 128
    CC = DH // 128
    QKC = 2 * CC
    NVT = SEQ // 128
    NCCH = 4 * HPC * DH // 128
    NTH = TC // 512

    nc = bass.Bass("TRN2", target_bir_lowering=False)

    x_t = nc.dram_tensor("x_t", [D, TC], F32, kind="ExternalInput")
    wqkvT = nc.dram_tensor("wqkvT", [D, 3 * HPC * DH], F32, kind="ExternalInput")
    wqks = nc.dram_tensor("wqks", [128, HPC * DC * 2 * DH], F32, kind="ExternalInput")
    wvs = nc.dram_tensor("wvs", [128, HPC * DC * DH], F32, kind="ExternalInput")
    wfs = nc.dram_tensor("wfs", [128, DC * DC * 128], F32, kind="ExternalInput")
    wos = nc.dram_tensor("wos", [128, DC * (4 * HPC * DH)], F32, kind="ExternalInput")
    bqkv = nc.dram_tensor("bqkv", [1, 3 * HPC * DH], F32, kind="ExternalInput")
    bqkv_pp = nc.dram_tensor("bqkv_pp", [128, 3 * HPC * DH // 128], F32, kind="ExternalInput")
    bo_pp = nc.dram_tensor("bo_pp", [128, D // 128], F32, kind="ExternalInput")
    bf_pp = nc.dram_tensor("bf_pp", [128, D // 128], F32, kind="ExternalInput")
    w_oT = nc.dram_tensor("w_oT", [4 * HPC * DH, D], F32, kind="ExternalInput")
    b_o = nc.dram_tensor("b_o", [1, D], F32, kind="ExternalInput")
    w_ffnT = nc.dram_tensor("w_ffnT", [D, D], F32, kind="ExternalInput")
    b_ffn = nc.dram_tensor("b_ffn", [1, D], F32, kind="ExternalInput")
    gateT = nc.dram_tensor("gateT", [D, E], F32, kind="ExternalInput")
    ics = nc.dram_tensor("ics", [E, D], F32, kind="ExternalInput")
    ocs = nc.dram_tensor("ocs", [E, D], F32, kind="ExternalInput")
    out_t = nc.dram_tensor("out_t", [D, TC], F32, kind="ExternalOutput")

    spill = nc.dram_tensor("spill", [128, DC * TC], F32, kind="Internal")
    wsign = nc.dram_tensor("wsign", [128, DC * DC * 128], BF16, kind="Internal")
    NH2 = TC // 512
    y2inL = [nc.dram_tensor("y2in%d" % i, [D, 512], BF16, kind="Internal")
             for i in range(NH2)]
    y2outL = [nc.dram_tensor("y2out%d" % i, [4 * D, 512], BF16, kind="Internal")
              for i in range(NH2)]
    vinL = [nc.dram_tensor("vin%d" % h, [DH, SEQ], BF16, kind="Internal")
            for h in range(HPC)]
    voutL = [nc.dram_tensor("vout%d" % h, [4 * DH, SEQ], BF16, kind="Internal")
             for h in range(HPC)]

    id_f32 = nc.inline_tensor(np.eye(128, dtype=np.float32), name="id_f32")
    ones_r = nc.inline_tensor(np.ones((1, 128), dtype=np.float32), name="ones_r")
    ones_cb = nc.inline_tensor(np.ones((128, 1), dtype=ml_dtypes.bfloat16), name="ones_cb")
    i1 = np.arange(128)[:, None]
    i2 = np.arange(512)[None, :]
    tS = np.stack([np.where(i2 <= k * 128 + i1, 0.0, MASK_NEG) for k in range(4)])
    tT = np.stack([np.where(k * 128 + i1 <= i2, 0.0, MASK_NEG) for k in range(4)])
    triS_c = nc.inline_tensor(tS.reshape(512, 512).astype(ml_dtypes.bfloat16), name="triS")
    triT_c = nc.inline_tensor(tT.reshape(512, 512).astype(ml_dtypes.bfloat16), name="triT")

    groups = [[0, 1, 2, 3], [4, 5, 6, 7]]
    ctx = Ctx(nc)

    with ExitStack() as st:
        for s in ("eT", "eV", "eS", "eG", "eY",
                  "dx", "dw0", "dw1", "dw2", "dws", "dy2", "dsp",
                  "dwh0", "dwh1", "dsl0", "dsl1", "dvs", "dvl",
                  "dwo0", "dwo1", "dwo2", "drl", "dout", "cc", "dsm", "dmk",
                  "dsg", "dn0", "dn1", "dn2", "dws0", "dws1", "dws2", "dvl1",
                  "dy20", "dy21", "dy22", "dy23", "dvs0", "dvs1"):
            ctx.mksem(st, s)
        wsems = ["dw0", "dw1", "dw2"]
        nsems = ["dn0", "dn1", "dn2"]
        wssems = ["dws0", "dws1", "dws2"]
        slsems = ["dsl0", "dsl1"]
        whsems = ["dwh0", "dwh1"]
        wosems = ["dwo0", "dwo1", "dwo2"]

        p_ident = st.enter_context(nc.sbuf_tensor("p_ident", [128, 128], F32))
        p_ones_r = st.enter_context(nc.sbuf_tensor("p_ones_r", [1, 128], F32))
        p_ones_cb = st.enter_context(nc.sbuf_tensor("p_ones_cb", [128, 1], BF16))
        p_ics = st.enter_context(nc.sbuf_tensor("p_ics", [E, D], BF16))
        p_ocs = st.enter_context(nc.sbuf_tensor("p_ocs", [E, D], BF16))
        p_gate = st.enter_context(nc.sbuf_tensor("p_gate", [128, DC * E], BF16))
        p_gsb = st.enter_context(nc.sbuf_tensor("p_gsb", [128, E], F32))
        p_bq = st.enter_context(nc.sbuf_tensor("p_bq", [128, 3 * HPC * DH // 128], F32))
        p_bo = st.enter_context(nc.sbuf_tensor("p_bo", [128, DC], F32))
        p_hbf = st.enter_context(nc.sbuf_tensor("p_hbf", [128, DC], F32))
        p_sm = st.enter_context(nc.sbuf_tensor("p_sm", [128, 1120], F32))

        blk = st.enter_context(nc.Block())

        ctx.op("S", lambda e: e.add_instruction(mybir.InstLoadActFuncSet(
            name=nc.get_next_instruction_name(), act_func_set_id=_act_table_idx(nc),
            ins=[], outs=[])), tick=False)
        ctx.dma("Y", "dsm", lambda e: e.dma_start(out=p_ident[:, :], in_=id_f32.ap()))
        ctx.dma("Y", "dsm", lambda e: e.dma_start(out=p_ones_r[:, :], in_=ones_r.ap()))
        ctx.dma("Y", "dsm", lambda e: e.dma_start(out=p_ones_cb[:, :], in_=ones_cb.ap()))
        ctx.dma("G", "dsg", lambda e: e.dma_start(out=p_ics[:, :], in_=ics[:, :]))
        ctx.dma("G", "dsg", lambda e: e.dma_start(out=p_ocs[:, :], in_=ocs[:, :]))
        ctx.dma("G", "dsg", lambda e: e.dma_start(
            out=p_gate[:, :].rearrange("p (dc e) -> p dc e", e=E),
            in_=gateT.ap().rearrange("(dc p) e -> p dc e", p=128)))
        ctx.dma("G", "dsg", lambda e: e.dma_start(out=p_bq[:, :], in_=bqkv_pp.ap()))
        ctx.dma("G", "dsg", lambda e: e.dma_start(out=p_bo[:, :], in_=bo_pp.ap()))
        ctx.dma("G", "dsg", lambda e: e.dma_start(out=p_hbf[:, :], in_=bf_pp.ap()))
        sm_all = ctx.dcnt["dsm"]
        sg_all = ctx.dcnt["dsg"]
        cHB = ctx.op("S", lambda e: e.mul(out=p_hbf[:, :], in_=p_hbf[:, :], mul=0.5),
                     waits=[("dsg", sg_all)])

        with ExitStack() as ges:
            pg0 = ges.enter_context(nc.psum_tensor("pg0", [1, 512], F32))
            pg1 = ges.enter_context(nc.psum_tensor("pg1", [128, 512], F32))
            for dc in range(DC):
                ctx.op("T", lambda e, dc=dc: e.matmul(
                    pg0[0:1, 0:E], p_ones_cb[:, :], p_gate[:, dc * E:(dc + 1) * E],
                    start=(dc == 0), stop=(dc == DC - 1)),
                    waits=[("dsm", sm_all), ("dsg", sg_all)], tick=(dc == DC - 1))
            tgs = ctx.cnt["T"]
            cV = ctx.op("V", lambda e: e.tensor_copy(out=p_sm[0:1, 0:E], in_=pg0[0:1, 0:E]),
                        waits=[("eT", tgs)])
            tgb = ctx.op("T", lambda e: e.matmul(pg1[:, 0:E], p_ones_r[:, :],
                                                 p_sm[0:1, 0:E], start=True, stop=True),
                         waits=[("eV", cV)])
            cGS = ctx.op("V", lambda e: e.tensor_copy(out=p_gsb[:, :], in_=pg1[:, 0:E]),
                         waits=[("eT", tgb)])
            ctx.flush(blk)

        def ffn_phase(phase, rsd, resid_marks, winit=None, with_ln2=False):
            px = "f%d_" % phase
            out_marks = {}
            ctx.barrier()
            with ExitStack() as fes:
                xx = fes.enter_context(nc.sbuf_tensor(px + "xx", [128, 2 * 2 * TH], BF16))
                z2 = fes.enter_context(nc.sbuf_tensor(px + "z", [128, 2 * DC * TH], BF16))
                wsl = fes.enter_context(nc.sbuf_tensor(px + "w", [128, 2 * DC * 128], BF16))
                rb = fes.enter_context(nc.sbuf_tensor(px + "rb", [128, TH], F32))
                routT = fes.enter_context(nc.sbuf_tensor(px + "routT", [E, 2 * TH], BF16))
                ris = fes.enter_context(nc.sbuf_tensor(px + "ris", [128, TH], F32))
                t1 = fes.enter_context(nc.sbuf_tensor(px + "t1", [128, TH], F32))
                osc = fes.enter_context(nc.sbuf_tensor(px + "osc", [128, TH], BF16))
                t2 = fes.enter_context(nc.sbuf_tensor(px + "t2", [128, TH], F32))
                y2b = (fes.enter_context(nc.sbuf_tensor(px + "y2b", [128, 4 * TH], BF16))
                       if with_ln2 else None)
                pst = fes.enter_context(nc.psum_tensor(px + "pst", [128, 512], F32))
                pl0 = fes.enter_context(nc.psum_tensor(px + "pl0", [128, 512], F32))
                pl1 = fes.enter_context(nc.psum_tensor(px + "pl1", [128, 512], F32))
                pis0 = fes.enter_context(nc.psum_tensor(px + "pis0", [128, 512], F32))
                pis1 = fes.enter_context(nc.psum_tensor(px + "pis1", [128, 512], F32))
                pm0 = fes.enter_context(nc.psum_tensor(px + "pm0", [128, 512], F32))
                pm1 = fes.enter_context(nc.psum_tensor(px + "pm1", [128, 512], F32))
                pos = fes.enter_context(nc.psum_tensor(px + "pos", [128, 512], F32))
                plog = [pl0, pl1]
                pis = [pis0, pis1]
                pm = [pm0, pm1]
                wfree = dict(winit or {})
                wstores = {}
                t1last = []
                xxfree = {}
                pmfree = {}
                posfree = []
                t2free = {}
                oscfree = []
                risfree = []
                t1free = []
                prev_c6 = 0
                pstfree = []
                plfree = []
                pisfree = {0: [], 1: []}
                routTfree = {0: [], 1: []}
                prestate = {}
                zstate = {}

                def emit_pre(q):
                    nonlocal prev_c6
                    b = 0
                    yield
                    negmean = p_sm[0:1, b + 0:b + TH]
                    rstd = p_sm[0:1, b + 256:b + 256 + TH]
                    ex2 = p_sm[0:1, 512:512 + TH]
                    m2 = p_sm[0:1, 768:768 + TH]
                    rb2 = 1024
                    for dc in range(DC):
                        if dc % 8 == 0:
                            yield
                        sl = dc % 2
                        xo = sl * 2 * TH
                        w = list(resid_marks.get((dc, q), [])) + xxfree.get(sl, [])
                        cs1 = ctx.op("S", lambda e, dc=dc, xo=xo, q=q: e.activation(
                            out=xx[:, xo:xo + TH],
                            in_=rsd[:, dc * TC + q * TH: dc * TC + q * TH + TH],
                            func=AF.Copy), waits=w)
                        cs2 = ctx.op("S", lambda e, dc=dc, xo=xo, q=q: e.activation(
                            out=xx[:, xo + TH:xo + 2 * TH],
                            in_=rsd[:, dc * TC + q * TH: dc * TC + q * TH + TH],
                            func=AF.Square))
                        ctx.op("T", lambda e, dc=dc, xo=xo: e.matmul(
                            pst[0:1, 0:2 * TH], p_ones_cb[:, :],
                            xx[:, xo:xo + 2 * TH],
                            start=(dc == 0), stop=(dc == DC - 1)),
                            waits=[("eS", cs2)] + (pstfree if dc == 0 else []),
                            tick=False)
                        for tt in range(NT):
                            tdc = ctx.op("T", lambda e, dc=dc, xo=xo, tt=tt: e.matmul(
                                plog[tt][:, 0:E],
                                xx[:, xo + tt * 128: xo + tt * 128 + 128],
                                p_gate[:, dc * E:(dc + 1) * E],
                                start=(dc == 0), stop=(dc == DC - 1)),
                                waits=(plfree if (dc == 0 and tt == 0) else []),
                                tick=(tt == NT - 1))
                        xxfree[sl] = [("eT", tdc)]
                    tsums = ctx.cnt["T"]
                    yield
                    c1 = ctx.op("V", lambda e, negmean=negmean: e.tensor_scalar(
                        out=negmean, in0=pst[0:1, 0:TH], scalar1=-1.0 / D, scalar2=None,
                        op0=ALU.mult), waits=[("eT", tsums), ("eS", prev_c6)])
                    c2 = ctx.op("V", lambda e, ex2=ex2: e.tensor_scalar(
                        out=ex2, in0=pst[0:1, TH:2 * TH], scalar1=1.0 / D, scalar2=None,
                        op0=ALU.mult), waits=[("eS", prev_c6)])
                    c3 = ctx.op("V", lambda e, negmean=negmean, m2=m2: e.tensor_mul(
                        out=m2, in0=negmean, in1=negmean), waits=[("eV", c1)])
                    c4 = ctx.op("V", lambda e, ex2=ex2, m2=m2: e.tensor_sub(
                        out=ex2, in0=ex2, in1=m2), waits=[("eV", c3), ("eV", c2)])

                    c4b = ctx.op("V", lambda e, ex2=ex2: e.tensor_scalar_add(
                        out=ex2, in0=ex2, scalar1=EPS), waits=[("eV", c4)])
                    c5 = ctx.op("S", lambda e, ex2=ex2, m2=m2: e.activation(
                        out=m2, in_=ex2, func=AF.Ln), waits=[("eV", c4b)])
                    c6 = ctx.op("S", lambda e, m2=m2, rstd=rstd: e.activation(
                        out=rstd, in_=m2, func=AF.Exp, scale=-0.5), waits=[("eS", c5)])
                    prev_c6 = c6
                    tb1 = ctx.op("T", lambda e, negmean=negmean: e.matmul(
                        pst[:, 0:TH], p_ones_r[:, :], negmean, start=True, stop=True),
                        waits=[("eV", c1), ("eV", c2)] + pstfree)
                    tb2 = ctx.op("T", lambda e, rstd=rstd: e.matmul(
                        pis0[:, TH:2 * TH], p_ones_r[:, :], rstd, start=True, stop=True),
                        waits=[("eS", c6)] + pisfree[0])
                    vb2 = ctx.op("V", lambda e: e.tensor_copy(out=rb[:, :], in_=pis0[:, TH:2 * TH]),
                                 waits=[("eT", tb2)])
                    tcl = []
                    for tt in range(NT):
                        tcl.append(ctx.op("T", lambda e, tt=tt, negmean=negmean: e.transpose(
                            pis1[:, tt:tt + 1], negmean[0:1, tt * 128:(tt + 1) * 128],
                            p_ident[0:1, 0:1]),
                            waits=[("eV", c1), ("eS", c6)] + (pisfree[1] if tt == 0 else [])))
                        tcl.append(ctx.op("T", lambda e, tt=tt, rstd=rstd: e.transpose(
                            pis1[:, NT + tt:NT + tt + 1], rstd[0:1, tt * 128:(tt + 1) * 128],
                            p_ident[0:1, 0:1])))
                    vcl = ctx.op("V", lambda e, rb2=rb2: e.tensor_copy(
                        out=p_sm[:, rb2:rb2 + 2 * NT], in_=pis1[:, 0:2 * NT]),
                        waits=[("eT", tcl[-1])])
                    rT_done = []
                    prev_tr = 0
                    for tt in range(NT):
                        yield
                        l0 = rb2 + 8
                        cr1 = ctx.op("V", lambda e, tt=tt, rb2=rb2, l0=l0: e.scalar_tensor_tensor(
                            out=p_sm[:, l0:l0 + E], in0=p_gsb[:, :],
                            scalar=p_sm[:, rb2 + tt:rb2 + tt + 1],
                            in1=plog[tt][:, 0:E], op0=ALU.mult, op1=ALU.add),
                            waits=[("eV", vcl), ("eV", cGS), ("eT", tsums),
                                   ("eT", prev_tr)])
                        cr2 = ctx.op("V", lambda e, tt=tt, rb2=rb2, l0=l0: e.tensor_scalar(
                            out=p_sm[:, l0:l0 + E], in0=p_sm[:, l0:l0 + E],
                            scalar1=p_sm[:, rb2 + NT + tt:rb2 + NT + tt + 1], scalar2=None,
                            op0=ALU.mult), waits=[("eV", cr1)])
                        cr3 = ctx.op("V", lambda e, rb2=rb2, l0=l0: e.reduce_max(
                            out=p_sm[:, rb2 + 4:rb2 + 5], in_=p_sm[:, l0:l0 + E],
                            axis=AX.X, negate=True), waits=[("eV", cr2)])
                        cr4 = ctx.op("S", lambda e, rb2=rb2, l0=l0: e.activation(
                            out=p_sm[:, l0 + 8:l0 + 8 + E], in_=p_sm[:, l0:l0 + E],
                            func=AF.Exp, bias=p_sm[:, rb2 + 4:rb2 + 5]),
                            waits=[("eV", cr3)])
                        cr5 = ctx.op("V", lambda e, rb2=rb2, l0=l0: e.reduce_sum(
                            out=p_sm[:, rb2 + 5:rb2 + 6], in_=p_sm[:, l0 + 8:l0 + 8 + E],
                            axis=AX.X), waits=[("eS", cr4)])
                        cr6 = ctx.op("V", lambda e, rb2=rb2: e.reciprocal(
                            out=p_sm[:, rb2 + 6:rb2 + 7], in_=p_sm[:, rb2 + 5:rb2 + 6]),
                            waits=[("eV", cr5)])
                        cr7 = ctx.op("V", lambda e, rb2=rb2, l0=l0: e.tensor_scalar(
                            out=p_sm[:, l0 + 8:l0 + 8 + E], in0=p_sm[:, l0 + 8:l0 + 8 + E],
                            scalar1=p_sm[:, rb2 + 6:rb2 + 7], scalar2=None, op0=ALU.mult),
                            waits=[("eV", cr6)])
                        prev_tr = ctx.op("T", lambda e, tt=tt, l0=l0: e.transpose(
                            pis1[0:E, 2 * NT + tt * 128: 2 * NT + (tt + 1) * 128],
                            p_sm[:, l0 + 8:l0 + 8 + E], p_ident[:, :]),
                            waits=[("eV", cr7)])
                        rT_done.append(prev_tr)
                    rofs = (q % 2) * TH
                    vrt = ctx.op("V", lambda e, rofs=rofs: e.tensor_copy(
                        out=routT[0:E, rofs:rofs + TH], in_=pis1[0:E, 2 * NT:2 * NT + TH]),
                        waits=[("eT", rT_done[-1])] + routTfree[q % 2])
                    pisfree[1] = [("eV", vrt)]
                    zofs = (q % 2) * DC * TH
                    zml = []
                    zwar = zstate.get(q - 2, (None, []))[1]
                    for dc in range(DC):
                        if dc % 8 == 0:
                            yield
                        sl = dc % 2
                        pb = pis[sl]
                        ti = ctx.op("T", lambda e, dc=dc, pb=pb, rofs=rofs: e.matmul(
                            pb[:, 0:TH], p_ics[0:E, dc * 128:(dc + 1) * 128],
                            routT[0:E, rofs:rofs + TH], start=True, stop=True),
                            waits=[("eV", vrt)] +
                            ([("eV", vb2)] if dc == 0 else []) + t1last[:])
                        v1 = ctx.op("V", lambda e, pb=pb: e.tensor_mul(
                            out=ris[:, 0:TH], in0=pb[:, 0:TH], in1=rb[:, :]),
                            waits=[("eT", ti), ("eV", vb2)] + t1last[:])
                        v2 = ctx.op("V", lambda e, dc=dc, q=q: e.tensor_add(
                            out=t1[:, 0:TH],
                            in0=rsd[:, dc * TC + q * TH: dc * TC + q * TH + TH],
                            in1=pst[:, 0:TH]),
                            waits=[("eT", tb1), ("eV", v1)] + t1last[:] +
                            list(resid_marks.get((dc, q), [])))
                        v3 = ctx.op("V", lambda e, dc=dc, zofs=zofs: e.tensor_mul(
                            out=z2[:, zofs + dc * TH: zofs + (dc + 1) * TH],
                            in0=t1[:, 0:TH], in1=ris[:, 0:TH]),
                            waits=[("eV", v2)] + zwar)
                        t1last[:] = [("eV", v3)]
                        zml.append([("eV", v3)])
                    pisfree[0] = [("eV", ctx.cnt["V"])]
                    pisfree[1] = [("eV", ctx.cnt["V"])]
                    plfree[:] = [("eV", ctx.cnt["V"])]
                    pstfree[:] = [("eV", ctx.cnt["V"])]
                    prestate[q] = (zml, vrt)

                def emit_main(q, zml, vrt):
                    zofs = (q % 2) * DC * TH
                    rofs = (q % 2) * TH
                    tmm = 0
                    tos = 0
                    tmm_of = {}

                    def epilogue(oc):
                        nonlocal tos
                        tmm_ = tmm_of[oc]
                        tos = ctx.op("T", lambda e, oc=oc, rofs=rofs: e.matmul(
                            pos[:, 0:TH], p_ocs[0:E, oc * 128:(oc + 1) * 128],
                            routT[0:E, rofs:rofs + TH], start=True, stop=True),
                            waits=[("eV", vrt)] + posfree)
                        so = ctx.op("S", lambda e, oc=oc: e.activation(
                            out=osc[:, :], in_=pos[:, 0:TH],
                            func=AF.Copy), waits=[("eT", tos)] + oscfree)
                        posfree[:] = [("eS", so)]
                        pb_ = pm[oc % 2]
                        v4 = ctx.op("V", lambda e, oc=oc, pb=pb_: e.tensor_mul(
                            out=t2[:, :], in0=pb_[:, 0:TH],
                            in1=osc[:, :]),
                            waits=[("eT", tmm_), ("eS", so)] + t2free.get(0, []))
                        pmfree[oc % 2] = [("eV", v4)]
                        oscfree[:] = [("eV", v4)]
                        s5 = ctx.op("V", lambda e, oc=oc: e.tensor_scalar(
                            out=t2[:, :], in0=t2[:, :],
                            scalar1=0.5, scalar2=p_hbf[:, oc:oc + 1],
                            op0=ALU.mult, op1=ALU.add),
                            waits=[("eV", v4), ("eS", cHB)])
                        v6 = ctx.op("V", lambda e, oc=oc, q=q: e.tensor_add(
                            out=rsd[:, oc * TC + q * TH: oc * TC + q * TH + TH],
                            in0=rsd[:, oc * TC + q * TH: oc * TC + q * TH + TH],
                            in1=t2[:, :]),
                            waits=[("eV", s5)])
                        t2free[0] = [("eV", v6)]
                        out_marks[(oc, q)] = [("eV", v6)]

                    for oc in range(DC):
                        slw = oc % 2
                        wofs = slw * DC * 128
                        if phase == 0 and q == 0:
                            ld = ctx.dma("G", wsems[slw], lambda e, oc=oc, wofs=wofs:
                                         e.dma_start(
                                out=wsl[:, wofs:wofs + DC * 128],
                                in_=wfs.ap()[:, oc * DC * 128:(oc + 1) * DC * 128]),
                                waits=wfree.get(slw, []))
                            sg = ctx.op("S", lambda e, wofs=wofs: e.activation(
                                out=wsl[:, wofs:wofs + DC * 128],
                                in_=wsl[:, wofs:wofs + DC * 128], func=AF.Sign),
                                waits=[(wsems[slw], ld)])
                            stw = ctx.dma("Y", wssems[slw], lambda e, oc=oc, wofs=wofs:
                                          e.dma_start(
                                out=wsign.ap()[:, oc * DC * 128:(oc + 1) * DC * 128],
                                in_=wsl[:, wofs:wofs + DC * 128]), waits=[("eS", sg)])
                            wrdy = [("eS", sg)]
                            wfree[slw] = [(wssems[slw], stw)]
                            wstores[oc] = (wssems[slw], stw)
                        else:
                            wts = [wstores[oc]] if (phase == 0 and oc in wstores) else []
                            ld = ctx.dma("Y", nsems[slw], lambda e, oc=oc, wofs=wofs:
                                         e.dma_start(
                                out=wsl[:, wofs:wofs + DC * 128],
                                in_=wsign.ap()[:, oc * DC * 128:(oc + 1) * DC * 128]),
                                waits=wfree.get(slw, []) + wts)
                            wrdy = [(nsems[slw], ld)]
                            wfree[slw] = []
                        pb = pm[oc % 2]
                        for dc in range(DC):
                            ctx.op("T", lambda e, oc=oc, dc=dc, pb=pb, wofs=wofs, zofs=zofs:
                                   e.matmul(
                                pb[:, 0:TH], wsl[:, wofs + dc * 128:wofs + dc * 128 + 128],
                                z2[:, zofs + dc * TH: zofs + (dc + 1) * TH],
                                start=(dc == 0), stop=(dc == DC - 1)),
                                waits=(wrdy + zml[dc] +
                                       (pmfree.get(oc % 2, []) if dc == 0 else [])),
                                tick=(dc == DC - 1))
                        tmm = ctx.cnt["T"]
                        tmm_of[oc] = tmm
                        if phase == 1:
                            wfree[slw] = [("eT", tmm)]
                        else:
                            wfree[slw] = wfree[slw] + [("eT", tmm)]
                        if oc >= 1:
                            epilogue(oc - 1)
                        yield
                    epilogue(DC - 1)
                    routTfree[q % 2] = [("eT", tos)]
                    zstate[q] = (zml, [("eT", tmm)])

                y2free = {}
                ln2state = {"c6": 0}

                def emit_ln2q(q):
                    yield
                    negmean = p_sm[0:1, 0:TH]
                    rstd = p_sm[0:1, 256:256 + TH]
                    ex2 = p_sm[0:1, 512:512 + TH]
                    m2 = p_sm[0:1, 768:768 + TH]
                    for dc in range(DC):
                        if dc % 8 == 0:
                            yield
                        sl = dc % 2
                        xo = sl * 2 * TH
                        w = list(out_marks.get((dc, q), [])) + xxfree.get(sl, [])
                        cs1 = ctx.op("S", lambda e, dc=dc, xo=xo, q=q: e.activation(
                            out=xx[:, xo:xo + TH],
                            in_=rsd[:, dc * TC + q * TH: dc * TC + q * TH + TH],
                            func=AF.Copy), waits=w)
                        cs2 = ctx.op("S", lambda e, dc=dc, xo=xo, q=q: e.activation(
                            out=xx[:, xo + TH:xo + 2 * TH],
                            in_=rsd[:, dc * TC + q * TH: dc * TC + q * TH + TH],
                            func=AF.Square))
                        tdc = ctx.op("T", lambda e, dc=dc, xo=xo: e.matmul(
                            pis1[0:1, 0:2 * TH], p_ones_cb[:, :],
                            xx[:, xo:xo + 2 * TH],
                            start=(dc == 0), stop=(dc == DC - 1)),
                            waits=[("eS", cs2)] + (pisfree[1] if dc == 0 else []),
                            tick=True)
                        xxfree[sl] = [("eT", tdc)]
                    tsums = ctx.cnt["T"]
                    yield
                    pc6 = ln2state["c6"]
                    c1 = ctx.op("V", lambda e, negmean=negmean: e.tensor_scalar(
                        out=negmean, in0=pis1[0:1, 0:TH], scalar1=-1.0 / D, scalar2=None,
                        op0=ALU.mult), waits=[("eT", tsums), ("eS", pc6)])
                    c2 = ctx.op("V", lambda e, ex2=ex2: e.tensor_scalar(
                        out=ex2, in0=pis1[0:1, TH:2 * TH], scalar1=1.0 / D, scalar2=None,
                        op0=ALU.mult), waits=[("eS", pc6)])
                    c3 = ctx.op("V", lambda e, negmean=negmean, m2=m2: e.tensor_mul(
                        out=m2, in0=negmean, in1=negmean), waits=[("eV", c1)])
                    c4 = ctx.op("V", lambda e, ex2=ex2, m2=m2: e.tensor_sub(
                        out=ex2, in0=ex2, in1=m2), waits=[("eV", c3), ("eV", c2)])
                    c4b = ctx.op("V", lambda e, ex2=ex2: e.tensor_scalar_add(
                        out=ex2, in0=ex2, scalar1=EPS), waits=[("eV", c4)])
                    c5 = ctx.op("S", lambda e, ex2=ex2, m2=m2: e.activation(
                        out=m2, in_=ex2, func=AF.Ln), waits=[("eV", c4b)])
                    c6 = ctx.op("S", lambda e, m2=m2, rstd=rstd: e.activation(
                        out=rstd, in_=m2, func=AF.Exp, scale=-0.5), waits=[("eS", c5)])
                    ln2state["c6"] = c6
                    tb1L = ctx.op("T", lambda e, negmean=negmean: e.matmul(
                        pis1[:, 0:TH], p_ones_r[:, :], negmean, start=True, stop=True),
                        waits=[("eV", c1), ("eV", c2)])
                    tb2L = ctx.op("T", lambda e, rstd=rstd: e.matmul(
                        pis1[:, TH:2 * TH], p_ones_r[:, :], rstd, start=True, stop=True),
                        waits=[("eS", c6)])
                    for dc in range(DC):
                        if dc % 8 == 0:
                            yield
                        sl4 = dc % 4
                        v2 = ctx.op("V", lambda e, dc=dc, q=q: e.tensor_add(
                            out=t1[:, 0:TH],
                            in0=rsd[:, dc * TC + q * TH: dc * TC + q * TH + TH],
                            in1=pis1[:, 0:TH]),
                            waits=[("eT", tb1L)] + t1last[:] +
                            list(out_marks.get((dc, q), [])))
                        v3 = ctx.op("V", lambda e, dc=dc, sl4=sl4: e.tensor_mul(
                            out=y2b[:, sl4 * TH:(sl4 + 1) * TH],
                            in0=t1[:, 0:TH], in1=pis1[:, TH:2 * TH]),
                            waits=[("eT", tb2L), ("eV", v2)] + y2free.get(sl4, []))
                        t1last[:] = [("eV", v3)]
                        dd = ctx.dma("Y", "dy2%d" % sl4, lambda e, dc=dc, q=q, sl4=sl4:
                                     e.dma_start(
                            out=y2inL[q // 2].ap()[dc * 128:(dc + 1) * 128,
                                                   (q % 2) * TH:(q % 2) * TH + TH],
                            in_=y2b[:, sl4 * TH:(sl4 + 1) * TH]), waits=[("eV", v3)])
                        y2free[sl4] = [("dy2%d" % sl4, dd)]
                        if q == NQ - 1:
                            ctx.dma("Y", "dsp", lambda e, dc=dc: e.dma_start(
                                out=spill.ap()[:, dc * TC:(dc + 1) * TC],
                                in_=rsd[:, dc * TC:(dc + 1) * TC]), waits=[("eV", v3)])
                    pisfree[1] = [("eV", ctx.cnt["V"])]
                    if q % 2 == 1:
                        h2 = q // 2
                        ctx.cc(y2inL[h2].ap().opt(), y2outL[h2].ap().opt(), groups,
                               waits=[("dy2%d" % i, ctx.dcnt["dy2%d" % i])
                                      for i in range(4)])
                        cct.append(ctx.dcnt["cc"])

                for q in range(NQ + 1):
                    pre_gen = emit_pre(q) if q < NQ else None
                    mn_gen = None
                    if q >= 1:
                        pz, pv = prestate[q - 1]
                        mn_gen = emit_main(q - 1, pz, pv)
                    alive_p = pre_gen is not None
                    alive_m = mn_gen is not None
                    while alive_p:
                        try:
                            next(pre_gen)
                        except StopIteration:
                            alive_p = False
                        if alive_m:
                            for _ in range(3):
                                try:
                                    next(mn_gen)
                                except StopIteration:
                                    alive_m = False
                                    break
                    ln_gen = emit_ln2q(q - 1) if (q >= 1 and with_ln2) else None
                    alive_l = ln_gen is not None
                    if alive_m:
                        for _ in range(9):
                            try:
                                next(mn_gen)
                            except StopIteration:
                                alive_m = False
                                break
                    while alive_m or alive_l:
                        if alive_l:
                            try:
                                next(ln_gen)
                            except StopIteration:
                                alive_l = False
                        if alive_m:
                            for _ in range(8):
                                try:
                                    next(mn_gen)
                                except StopIteration:
                                    alive_m = False
                                    break
                ctx.flush(blk)
            return out_marks

        cct = []

        def ln2_phase(rsd, resid_marks):
            ctx.barrier()
            with ExitStack() as les:
                xb = les.enter_context(nc.sbuf_tensor("l_xb", [128, 2 * TH], BF16))
                xsq = les.enter_context(nc.sbuf_tensor("l_xsq", [128, 2 * TH], BF16))
                nmb = les.enter_context(nc.sbuf_tensor("l_nmb", [128, TH], F32))
                rb = les.enter_context(nc.sbuf_tensor("l_rb", [128, TH], F32))
                y2b = les.enter_context(nc.sbuf_tensor("l_y2", [128, 4 * TH], BF16))
                t1 = les.enter_context(nc.sbuf_tensor("l_t1", [128, 2 * TH], F32))
                pSx = les.enter_context(nc.psum_tensor("l_pSx", [1, 512], F32))
                pSx2 = les.enter_context(nc.psum_tensor("l_pSx2", [1, 512], F32))
                pB = les.enter_context(nc.psum_tensor("l_pB", [128, 512], F32))
                prev_c6 = 0
                xbfree = {}
                t1free = {}
                y2free = {}
                for q in range(NQ):
                    b = 0
                    negmean = p_sm[0:1, b + 0:b + TH]
                    rstd = p_sm[0:1, b + 256:b + 256 + TH]
                    ex2 = p_sm[0:1, 512:512 + TH]
                    m2 = p_sm[0:1, 768:768 + TH]
                    for dc in range(DC):
                        sl = dc % 2
                        w = list(resid_marks.get((dc, q), [])) + xbfree.get(sl, [])
                        cs1 = ctx.op("S", lambda e, dc=dc, sl=sl, q=q: e.activation(
                            out=xb[:, sl * TH:(sl + 1) * TH],
                            in_=rsd[:, dc * TC + q * TH: dc * TC + q * TH + TH],
                            func=AF.Copy), waits=w)
                        cs2 = ctx.op("S", lambda e, dc=dc, sl=sl, q=q: e.activation(
                            out=xsq[:, sl * TH:(sl + 1) * TH],
                            in_=rsd[:, dc * TC + q * TH: dc * TC + q * TH + TH],
                            func=AF.Square))
                        ctx.op("T", lambda e, dc=dc, sl=sl: e.matmul(
                            pSx[0:1, 0:TH], p_ones_cb[:, :], xb[:, sl * TH:(sl + 1) * TH],
                            start=(dc == 0), stop=(dc == DC - 1)),
                            waits=[("eS", cs1)], tick=False)
                        tdc = ctx.op("T", lambda e, dc=dc, sl=sl: e.matmul(
                            pSx2[0:1, 0:TH], p_ones_cb[:, :], xsq[:, sl * TH:(sl + 1) * TH],
                            start=(dc == 0), stop=(dc == DC - 1)),
                            waits=[("eS", cs2)], tick=True)
                        xbfree[sl] = [("eT", tdc)]
                    tsums = ctx.cnt["T"]
                    c1 = ctx.op("V", lambda e, negmean=negmean: e.tensor_scalar(
                        out=negmean, in0=pSx[0:1, 0:TH], scalar1=-1.0 / D, scalar2=None,
                        op0=ALU.mult), waits=[("eT", tsums), ("eS", prev_c6)])
                    c2 = ctx.op("V", lambda e, ex2=ex2: e.tensor_scalar(
                        out=ex2, in0=pSx2[0:1, 0:TH], scalar1=1.0 / D, scalar2=None,
                        op0=ALU.mult), waits=[("eS", prev_c6)])
                    c3 = ctx.op("V", lambda e, negmean=negmean, m2=m2: e.tensor_mul(
                        out=m2, in0=negmean, in1=negmean), waits=[("eV", c1)])
                    c4 = ctx.op("V", lambda e, ex2=ex2, m2=m2: e.tensor_sub(
                        out=ex2, in0=ex2, in1=m2), waits=[("eV", c3), ("eV", c2)])
                    c4b = ctx.op("V", lambda e, ex2=ex2: e.tensor_scalar_add(
                        out=ex2, in0=ex2, scalar1=EPS), waits=[("eV", c4)])
                    c5 = ctx.op("S", lambda e, ex2=ex2, m2=m2: e.activation(
                        out=m2, in_=ex2, func=AF.Ln), waits=[("eV", c4b)])
                    c6 = ctx.op("S", lambda e, m2=m2, rstd=rstd: e.activation(
                        out=rstd, in_=m2, func=AF.Exp, scale=-0.5), waits=[("eS", c5)])
                    prev_c6 = c6
                    tb1 = ctx.op("T", lambda e, negmean=negmean: e.matmul(
                        pB[:, 0:TH], p_ones_r[:, :], negmean, start=True, stop=True),
                        waits=[("eV", c1)])
                    vb1 = ctx.op("V", lambda e: e.tensor_copy(out=nmb[:, :], in_=pB[:, 0:TH]),
                                 waits=[("eT", tb1)])
                    tb2 = ctx.op("T", lambda e, rstd=rstd: e.matmul(
                        pB[:, TH:2 * TH], p_ones_r[:, :], rstd, start=True, stop=True),
                        waits=[("eS", c6)])
                    vb2 = ctx.op("V", lambda e: e.tensor_copy(out=rb[:, :], in_=pB[:, TH:2 * TH]),
                                 waits=[("eT", tb2)])
                    for dc in range(DC):
                        sl = dc % 2
                        sl4 = dc % 4
                        v2 = ctx.op("V", lambda e, dc=dc, sl=sl, q=q: e.tensor_add(
                            out=t1[:, sl * TH:(sl + 1) * TH],
                            in0=rsd[:, dc * TC + q * TH: dc * TC + q * TH + TH],
                            in1=nmb[:, :]),
                            waits=[("eV", vb1)] + t1free.get(sl, []) +
                            list(resid_marks.get((dc, q), [])))
                        v3 = ctx.op("V", lambda e, dc=dc, sl=sl, sl4=sl4: e.tensor_mul(
                            out=y2b[:, sl4 * TH:(sl4 + 1) * TH],
                            in0=t1[:, sl * TH:(sl + 1) * TH], in1=rb[:, :]),
                            waits=[("eV", v2)] + y2free.get(sl4, []) + [("eV", vb2)])
                        t1free[sl] = [("eV", v3)]
                        dd = ctx.dma("Y", "dy2%d" % sl4, lambda e, dc=dc, q=q, sl4=sl4:
                                     e.dma_start(
                            out=y2inL[q // 2].ap()[dc * 128:(dc + 1) * 128,
                                                   (q % 2) * TH:(q % 2) * TH + TH],
                            in_=y2b[:, sl4 * TH:(sl4 + 1) * TH]), waits=[("eV", v3)])
                        y2free[sl4] = [("dy2%d" % sl4, dd)]
                        if q == NQ - 1:
                            ctx.dma("Y", "dsp", lambda e, dc=dc: e.dma_start(
                                out=spill.ap()[:, dc * TC:(dc + 1) * TC],
                                in_=rsd[:, dc * TC:(dc + 1) * TC]), waits=[("eV", v3)])
                    if q % 2 == 1:
                        h2 = q // 2
                        ctx.cc(y2inL[h2].ap().opt(), y2outL[h2].ap().opt(), groups,
                               waits=[("dy2%d" % i, ctx.dcnt["dy2%d" % i])
                                      for i in range(4)])
                        cct.append(ctx.dcnt["cc"])
                ctx.flush(blk)

        with nc.sbuf_tensor("residA", [128, DC * TC], F32) as residA:
            ctx.dma("Y", "dx", lambda e: [e.dma_start(
                out=residA[:, i * (DC // 4) * TC:(i + 1) * (DC // 4) * TC]
                .rearrange("p (dc t) -> p dc t", t=TC),
                in_=x_t.ap()[i * (D // 4):(i + 1) * (D // 4), :]
                .rearrange("(dc p) t -> p dc t", p=128)) for i in range(4)], n=4)
            xin = ctx.dcnt["dx"]
            xmarks = {(dc, q): [("dx", xin)] for dc in range(DC) for q in range(NQ)}
            ctx.flush(blk)
            if stages >= 1:
                rm = ffn_phase(0, residA, xmarks, with_ln2=(stages >= 2))
            else:
                rm = xmarks
            if stages == 1:
                ctx.dma("Y", "dout", lambda e: [e.dma_start(
                    out=out_t.ap()[i * (D // 4):(i + 1) * (D // 4), :]
                    .rearrange("(dc p) t -> p dc t", p=128),
                    in_=residA[:, i * (DC // 4) * TC:(i + 1) * (DC // 4) * TC]
                    .rearrange("p (dc t) -> p dc t", t=TC))
                    for i in range(4)],
                    waits=[("eV", ctx.cnt["V"]), ("eS", ctx.cnt["S"])], n=4)
                ctx.op("G", lambda e: e.nop(), waits=[("dout", ctx.dcnt["dout"])], tick=False)
                ctx.flush(blk)
                return nc
            ctx.flush(blk)
        aT, aV, aS = ctx.cnt["T"], ctx.cnt["V"], ctx.cnt["S"]
        spdone = ctx.dcnt["dsp"]
        ph_gate = ([("dsp", spdone), ("eT", aT), ("eV", aV), ("eS", aS)] +
                   [("dy2%d" % i, ctx.dcnt["dy2%d" % i]) for i in range(4)])

        am = {}
        ccvt = []
        ctx.barrier()
        with ExitStack() as aes:
            wqk = aes.enter_context(nc.sbuf_tensor("a_wqk", [128, DC * 2 * DH], BF16))
            wv = aes.enter_context(nc.sbuf_tensor("a_wv", [128, DC * DH], BF16))
            Kh = aes.enter_context(nc.sbuf_tensor("a_K", [128, CC * SEQ], BF16))
            Vh = aes.enter_context(nc.sbuf_tensor("a_V", [128, NVT * DH], BF16))
            Qc = aes.enter_context(nc.sbuf_tensor("a_Q", [128, 2 * CC * 512], BF16))
            y2b2 = aes.enter_context(nc.sbuf_tensor("a_y2b", [128, 2 * DC * 512], BF16))
            triS = aes.enter_context(nc.sbuf_tensor("a_triS", [128, 4 * 512], BF16))
            triT = aes.enter_context(nc.sbuf_tensor("a_triT", [128, 4 * 512], BF16))
            stmp = aes.enter_context(nc.sbuf_tensor("a_stmp", [128, 3 * 512], F32))
            ptb = aes.enter_context(nc.sbuf_tensor("a_pt", [128, 4 * 512], BF16))
            nmb2 = aes.enter_context(nc.sbuf_tensor("a_nmb", [128, 512], F32))
            rcb = aes.enter_context(nc.sbuf_tensor("a_rcb", [128, 512], F32))
            bvb = aes.enter_context(nc.sbuf_tensor("a_bvb", [128, DH], F32))
            vo = aes.enter_context(nc.sbuf_tensor("a_vo", [128, 2 * 512], BF16))
            p_bqf = aes.enter_context(nc.sbuf_tensor("a_bqf", [1, 3 * HPC * DH], F32))
            PS = aes.enter_context(nc.psum_tensor("a_p", [128, 8 * 512], F32))

            def bank(bi, w=512):
                return PS[:, bi * 512: bi * 512 + w]
            ctx.dma("Y", "dmk", lambda e: e.dma_start(
                out=triS[:, :].rearrange("p (k f) -> p k f", f=512),
                in_=triS_c.ap().rearrange("(k p) f -> p k f", p=128)),
                waits=ph_gate)
            ctx.dma("Y", "dmk", lambda e: e.dma_start(out=p_bqf[:, :], in_=bqkv.ap()))
            ctx.dma("Y", "dmk", lambda e: e.dma_start(
                out=triT[:, :].rearrange("p (k f) -> p k f", f=512),
                in_=triT_c.ap().rearrange("(k p) f -> p k f", p=128)))
            mkld = ctx.dcnt["dmk"]
            ev = {0: [], 1: [], 2: []}
            gctr = 0
            prevh = {}
            y2m = {}
            y2war = {0: [], 1: []}
            qcwar = {0: [], 1: []}
            smf = {2: [], 3: [], 4: [], 5: [], 6: [], 7: []}
            NJ = DC // 4

            def load_y2blk(h, tb):
                par = tb % 2
                rank = (tb * 512) // TC
                h2 = ((tb * 512) % TC) // 512
                ld = ctx.dma("Y", slsems[par], lambda e, par=par, rank=rank, h2=h2: [
                    e.dma_start(
                        out=y2b2[:, par * DC * 512 + i * (DC // 4) * 512:
                                 par * DC * 512 + (i + 1) * (DC // 4) * 512]
                        .rearrange("p (dc t) -> p dc t", t=512),
                        in_=y2outL[h2].ap()[rank * D + i * (D // 4):
                                            rank * D + (i + 1) * (D // 4), :]
                        .rearrange("(dc p) t -> p dc t", p=128)) for i in range(4)],
                    waits=y2war[par] + [("cc", cct[h2])], n=4)
                y2m[par] = [(slsems[par], ld)]

            def emit_qkv(h, tb, wrdy, vb0mark):
                nonlocal gctr
                par = tb % 2
                wqofs = 0
                wvofs = 0
                yofs = par * DC * 512
                groups = [("qk", rc) for rc in range(QKC)] + [("v", tt) for tt in range(4)]
                gend = {}
                for kind, idx in groups:
                    yield
                    b = gctr % 2
                    gctr += 1
                    for dc in range(DC):
                        if kind == "qk":
                            ctx.op("T", lambda e, b=b, idx=idx, dc=dc, wqofs=wqofs, yofs=yofs:
                                   e.matmul(
                                bank(b),
                                wqk[:, wqofs + dc * 2 * DH + idx * 128:
                                     wqofs + dc * 2 * DH + idx * 128 + 128],
                                y2b2[:, yofs + dc * 512: yofs + (dc + 1) * 512],
                                start=(dc == 0), stop=(dc == DC - 1)),
                                waits=(wrdy + y2m[par] + (ev[b] if dc == 0 else [])),
                                tick=(dc == DC - 1))
                        else:
                            ctx.op("T", lambda e, b=b, idx=idx, dc=dc, wvofs=wvofs, yofs=yofs:
                                   e.matmul(
                                bank(b, DH),
                                y2b2[:, yofs + dc * 512 + idx * 128:
                                     yofs + dc * 512 + idx * 128 + 128],
                                wv[:, wvofs + dc * DH: wvofs + (dc + 1) * DH],
                                start=(dc == 0), stop=(dc == DC - 1)),
                                waits=(wrdy + y2m[par] + (ev[b] if dc == 0 else [])),
                                tick=(dc == DC - 1))
                    tg = ctx.cnt["T"]
                    if kind == "qk":
                        rc = idx
                        bcol = (h * CC + rc) if rc < CC else (HPC * CC + h * CC + rc - CC)
                        sv = ctx.op("V", lambda e, b=b, rc=rc, bcol=bcol, tb=tb, par=par:
                                    e.tensor_scalar(
                            out=(Qc[:, par * CC * 512 + rc * 512:
                                    par * CC * 512 + (rc + 1) * 512] if rc < CC else
                                 Kh[:, (rc - CC) * SEQ + tb * 512:
                                    (rc - CC) * SEQ + tb * 512 + 512]),
                            in0=bank(b), scalar1=p_bq[:, bcol:bcol + 1], scalar2=None,
                            op0=ALU.add),
                            waits=[("eT", tg)] + (qcwar[par] if rc < CC else []))
                        ev[b] = [("eV", sv)]
                    else:
                        tt = idx
                        vv = ctx.op("V", lambda e, b=b, tt=tt, tb=tb: e.tensor_add(
                            out=Vh[:, (tb * 4 + tt) * DH:(tb * 4 + tt + 1) * DH],
                            in0=bank(b, DH), in1=bvb[:, :]),
                            waits=[("eT", tg)] + vb0mark)
                        ev[b] = [("eV", vv)]
                    gend[(kind, idx)] = ev[b]
                am[("kvrdy", tb)] = [("eV", ctx.cnt["V"])]
                y2war[tb % 2] = [("eT", ctx.cnt["T"])]

            def emit_softmax(h, tb):
                par = tb % 2
                qofs = par * CC * 512
                kvrdy = am[("kvrdy", tb)]
                nm4 = p_sm[:, 1056:1060]
                rmx = p_sm[:, 1064:1065]
                tmx = p_sm[:, 1065:1066]
                vn = 0
                nm4_war = am.get("nm4_read", [])
                mx = p_sm[:, 1072:1072 + 40]
                units = [(tt, fb) for tt in range(4) for fb in range(tb + 1)]
                nun = len(units)
                mxw = am.get("mxfree", [])
                for i in range(nun + 1):
                    if i < nun:
                        tt, fb = units[i]
                        pb = 2 + (i % 3)
                        if fb == 0:
                            yield
                        for cc2 in range(CC):
                            ctx.op("T", lambda e, tt=tt, fb=fb, cc2=cc2, pb=pb, qofs=qofs:
                                   e.matmul(
                                bank(pb),
                                Qc[:, qofs + cc2 * 512 + tt * 128:
                                   qofs + cc2 * 512 + tt * 128 + 128],
                                Kh[:, cc2 * SEQ + fb * 512: cc2 * SEQ + fb * 512 + 512],
                                start=(cc2 == 0), stop=(cc2 == CC - 1)),
                                waits=(kvrdy + [("dmk", mkld)] +
                                       (smf[pb] if cc2 == 0 else [])),
                                tick=(cc2 == CC - 1))
                        am[("smm", i)] = ctx.cnt["T"]
                    if i >= 1:
                        tt, fb = units[i - 1]
                        pb = 2 + ((i - 1) % 3)
                        tS_ = am[("smm", i - 1)]
                        if fb == tb:
                            va = ctx.op("V", lambda e, tt=tt, pb=pb: e.tensor_add(
                                out=stmp[:, (pb - 2) * 512:(pb - 1) * 512], in0=bank(pb),
                                in1=triS[:, tt * 512:(tt + 1) * 512]),
                                waits=[("eT", tS_)] + am.get(("stfree", pb), []))
                            am[("stfree", pb)] = [("eV", va)]
                            vm = ctx.op("V", lambda e, pb=pb, i=i: e.tensor_reduce(
                                out=mx[:, i - 1:i], in_=stmp[:, (pb - 2) * 512:(pb - 1) * 512],
                                axis=AX.X, op=ALU.max),
                                waits=[("eV", va)] + mxw)
                        else:
                            vm = ctx.op("V", lambda e, pb=pb, i=i: e.tensor_reduce(
                                out=mx[:, i - 1:i], in_=bank(pb), axis=AX.X, op=ALU.max),
                                waits=[("eT", tS_)] + mxw)
                        smf[pb] = [("eV", vm)]
                vlast = ctx.cnt["V"]
                for tt in range(4):
                    cols = [i for i, (t2_, _) in enumerate(units) if t2_ == tt]
                    a0, a1 = cols[0], cols[-1] + 1
                    vn = ctx.op("V", lambda e, tt=tt, a0=a0, a1=a1: e.tensor_reduce(
                        out=nm4[:, tt:tt + 1], in_=mx[:, a0:a1], axis=AX.X, op=ALU.max,
                        negate=True), waits=[("eV", vlast)] + nm4_war)
                am["mxfree"] = [("eV", ctx.cnt["V"])]
                yield
                tfl = 0
                for tt in range(4):
                    tfl = ctx.op("T", lambda e, tt=tt: e.transpose(
                        bank(5)[0:1, tt * 128:(tt + 1) * 128], nm4[:, tt:tt + 1],
                        p_ident[:, :]),
                        waits=[("eV", vn)] + (smf[5] if tt == 0 else []))
                am["nm4_read"] = [("eT", tfl)]
                vfl = ctx.op("V", lambda e: e.tensor_copy(
                    out=p_sm[0:1, 0:512], in_=bank(5)[0:1, 0:512]), waits=[("eT", tfl)])
                tbc = ctx.op("T", lambda e: e.matmul(
                    bank(5), p_ones_r[:, :], p_sm[0:1, 0:512], start=True, stop=True),
                    waits=[("eV", vfl)])
                vbc = ctx.op("V", lambda e: e.tensor_copy(out=nmb2[:, :], in_=bank(5)),
                             waits=[("eT", tbc)])
                nfc = 4 * (tb + 1)

                def st_mm(fc):
                    pb = 2 + (fc % 3)
                    for cc2 in range(CC):
                        ctx.op("T", lambda e, fc=fc, cc2=cc2, pb=pb, qofs=qofs: e.matmul(
                            bank(pb),
                            Kh[:, cc2 * SEQ + fc * 128: cc2 * SEQ + fc * 128 + 128],
                            Qc[:, qofs + cc2 * 512: qofs + (cc2 + 1) * 512],
                            start=(cc2 == 0), stop=(cc2 == CC - 1)),
                            waits=(smf[pb] if cc2 == 0 else []),
                            tick=(cc2 == CC - 1))
                    am[("stmm", fc)] = ctx.cnt["T"]

                def st_tail(fc):
                    pb = 2 + (fc % 3)
                    tST = am[("stmm", fc)]
                    va = ctx.op("V", lambda e, pb=pb: e.tensor_add(
                        out=stmp[:, (pb - 2) * 512:(pb - 1) * 512], in0=bank(pb),
                        in1=nmb2[:, :]),
                        waits=[("eT", tST), ("eV", vbc)] + am.get(("stfree", pb), []))
                    smf[pb] = [("eV", va)]
                    if fc >= tb * 4:
                        kk = fc - tb * 4
                        va = ctx.op("V", lambda e, pb=pb, kk=kk: e.tensor_add(
                            out=stmp[:, (pb - 2) * 512:(pb - 1) * 512],
                            in0=stmp[:, (pb - 2) * 512:(pb - 1) * 512],
                            in1=triT[:, kk * 512:(kk + 1) * 512]), waits=[("eV", va)])
                    sp = ctx.op("S", lambda e, fc=fc, pb=pb: e.activation(
                        out=ptb[:, (fc % 4) * 512:((fc % 4) + 1) * 512],
                        in_=stmp[:, (pb - 2) * 512:(pb - 1) * 512], func=AF.Exp),
                        waits=[("eV", va)] + am.get(("ptfree", fc % 4), []))
                    am[("stfree", pb)] = [("eS", sp)]
                    ctx.op("T", lambda e, fc=fc, nfc=nfc: e.matmul(
                        bank(5)[0:1, 0:512], p_ones_cb[:, :],
                        ptb[:, (fc % 4) * 512:((fc % 4) + 1) * 512],
                        start=(fc == 0), stop=(fc == nfc - 1)),
                        waits=[("eS", sp)] + ([("eV", vb0)] if fc == 0 else []),
                        tick=(fc == nfc - 1))
                    for cc2 in range(CC):
                        ctx.op("T", lambda e, fc=fc, cc2=cc2, nfc=nfc: e.matmul(
                            bank(6 + cc2),
                            Vh[:, fc * DH + cc2 * 128: fc * DH + cc2 * 128 + 128],
                            ptb[:, (fc % 4) * 512:((fc % 4) + 1) * 512],
                            start=(fc == 0), stop=(fc == nfc - 1)),
                            waits=([("eS", sp)] + (smf[6 + cc2] if fc == 0 else [])),
                            tick=(fc == nfc - 1))
                        if fc == nfc - 1:
                            am[("attdone", cc2)] = [("eT", ctx.cnt["T"])]
                    am[("ptfree", fc % 4)] = [("eT", ctx.cnt["T"])]

                for fc in range(nfc + 1):
                    if fc < nfc:
                        if fc % 2 == 0:
                            yield
                        st_mm(fc)
                    if fc >= 1:
                        st_tail(fc - 1)
                yield
                tsum = ctx.cnt["T"]
                vrc = ctx.op("V", lambda e: e.reciprocal(
                    out=p_sm[0:1, 512:1024], in_=bank(5)[0:1, 0:512]),
                    waits=[("eT", tsum)])
                smf[5] = [("eV", vrc)]
                trb = ctx.op("T", lambda e: e.matmul(
                    bank(2), p_ones_r[:, :], p_sm[0:1, 512:1024], start=True, stop=True),
                    waits=[("eV", vrc)] + smf[2])
                vrb = ctx.op("V", lambda e: e.tensor_copy(out=rcb[:, :], in_=bank(2)),
                             waits=[("eT", trb)])
                smf[2] = [("eV", vrb)]
                for cc2 in range(CC):
                    vf = ctx.op("V", lambda e, cc2=cc2: e.tensor_mul(
                        out=vo[:, cc2 * 512:(cc2 + 1) * 512], in0=bank(6 + cc2),
                        in1=rcb[:, :]),
                        waits=([("eV", vrb)] + am[("attdone", cc2)] +
                               am.get(("vofree", cc2), [])))
                    smf[6 + cc2] = [("eV", vf)]
                    dv = ctx.dma("Y", "dvs%d" % cc2, lambda e, h=h, cc2=cc2, tb=tb:
                                 e.dma_start(
                        out=vinL[h].ap()[cc2 * 128:(cc2 + 1) * 128,
                                         tb * 512:(tb + 1) * 512],
                        in_=vo[:, cc2 * 512:(cc2 + 1) * 512]), waits=[("eV", vf)])
                    am[("vofree", cc2)] = [("dvs%d" % cc2, dv)]
                qcwar[par] = [("eT", ctx.cnt["T"])]
                y2war[par] = y2war[par]

            for h in range(HPC):
                hb = h % 2
                wld = ctx.dma("G", "dwh0", lambda e, h=h: e.dma_start(
                    out=wqk[:, :],
                    in_=wqks.ap()[:, h * DC * 2 * DH:(h + 1) * DC * 2 * DH]),
                    waits=ph_gate + prevh.get("wqkfree", []))
                wld = ctx.dma("G", "dwh0", lambda e, h=h: e.dma_start(
                    out=wv[:, :],
                    in_=wvs.ap()[:, h * DC * DH:(h + 1) * DC * DH]))
                wrdy = [("dwh0", wld)]
                tb0 = ctx.op("T", lambda e, h=h: e.matmul(
                    bank(5, DH), p_ones_r[:, :],
                    p_bqf[0:1, 2 * HPC * DH + h * DH: 2 * HPC * DH + h * DH + DH],
                    start=True, stop=True),
                    waits=[("dsm", sm_all), ("dsg", sg_all), ("dmk", ctx.dcnt["dmk"])] +
                    smf[5] + ph_gate)
                vb0 = ctx.op("V", lambda e: e.tensor_copy(out=bvb[:, :], in_=bank(5, DH)),
                             waits=[("eT", tb0)] + prevh.get("bvbfree", []))
                smf[5] = [("eV", vb0)]
                vb0mark = [("eV", vb0)]
                load_y2blk(h, 0)
                for u in range(NTB + 1):
                    qk_gen = emit_qkv(h, u, wrdy, vb0mark) if u < NTB else None
                    sm_gen = emit_softmax(h, u - 1) if u >= 1 else None
                    alive_q, alive_s = qk_gen is not None, sm_gen is not None
                    while alive_q or alive_s:
                        if alive_q:
                            try:
                                next(qk_gen)
                            except StopIteration:
                                alive_q = False
                                if u + 1 < NTB:
                                    load_y2blk(h, u + 1)
                        if alive_s:
                            try:
                                next(sm_gen)
                            except StopIteration:
                                alive_s = False
                prevh["wqkfree"] = [("eT", ctx.cnt["T"])]
                prevh["bvbfree"] = [("eV", ctx.cnt["V"])]
                ctx.cc(vinL[h].ap().opt(), voutL[h].ap().opt(), groups,
                       waits=[("dvs%d" % i, ctx.dcnt["dvs%d" % i]) for i in range(CC)])
                ccvt.append(ctx.dcnt["cc"])
                ctx.flush(blk)
            ctx.flush(blk)
        bT, bV, bS = ctx.cnt["T"], ctx.cnt["V"], ctx.cnt["S"]
        ph2 = [("eT", bT), ("eV", bV), ("eS", bS)]

        with nc.sbuf_tensor("residB", [128, DC * TC], F32) as residB:
            rl = ctx.dma("Y", "drl", lambda e: [e.dma_start(
                out=residB[:, i * (DC // 4) * TC:(i + 1) * (DC // 4) * TC],
                in_=spill.ap()[:, i * (DC // 4) * TC:(i + 1) * (DC // 4) * TC])
                for i in range(4)], waits=ph2 + [("dsp", spdone)], n=4)
            opm = {}
            ctx.barrier()
            with ExitStack() as ces:
                valsr = ces.enter_context(nc.sbuf_tensor("c_vals", [128, NCCH * NTH * 512], BF16))
                wo = ces.enter_context(nc.sbuf_tensor("c_wo", [128, 2 * NCCH * 128], BF16))
                t4 = ces.enter_context(nc.sbuf_tensor("c_t4", [128, 2 * 512], F32))
                cp0 = ces.enter_context(nc.psum_tensor("c_pm0", [128, 512], F32))
                cp1 = ces.enter_context(nc.psum_tensor("c_pm1", [128, 512], F32))
                cp = [cp0, cp1]
                NCH = 4 * DH // 128
                vlt = {}
                for h in range(HPC):
                    def vload(e, h=h):
                        pid = e.partition_id()
                        off = e.snap((pid % 4) * TC, min_val=0, max_val=3 * TC)
                        return [e.dma_start(
                            out=valsr[:, (h * NCH + c) * NTH * 512:
                                      (h * NCH + c + 1) * NTH * 512],
                            in_=voutL[h].ap()[c * 128:(c + 1) * 128, :][:, bass.ds(off, TC)])
                            for c in range(NCH)]
                    sem = "dvl" if h == 0 else "dvl1"
                    ctx.dma("Y", sem, vload, waits=ph2 + [("cc", ccvt[h])], n=NCH)
                    vlt[h] = (sem, ctx.dcnt[sem])
                wofree = {}
                cfree = {}
                t4free = {}
                pend = []

                def op_epi(oc, th, tmm):
                    par = (oc * NTH + th) % 2
                    pb = cp[par]
                    s1 = ctx.op("V", lambda e, oc=oc, par=par, pb=pb: e.tensor_scalar(
                        out=t4[:, par * 512:(par + 1) * 512], in0=pb[:, 0:512],
                        scalar1=p_bo[:, oc:oc + 1], scalar2=None, op0=ALU.add),
                        waits=[("eT", tmm)] + t4free.get(par, []))
                    cfree[par] = [("eV", s1)]
                    v1 = ctx.op("V", lambda e, oc=oc, th=th, par=par: e.tensor_add(
                        out=residB[:, oc * TC + th * 512: oc * TC + th * 512 + 512],
                        in0=residB[:, oc * TC + th * 512: oc * TC + th * 512 + 512],
                        in1=t4[:, par * 512:(par + 1) * 512]),
                        waits=[("eV", s1), ("drl", rl)])
                    t4free[par] = [("eV", v1)]
                    for qq in (2 * th, 2 * th + 1):
                        opm[(oc, qq)] = [("eV", v1)]

                for oc in range(DC):
                    slw = oc % 2
                    wofs = slw * NCCH * 128
                    ld = ctx.dma("G", wosems[slw], lambda e, oc=oc, wofs=wofs: e.dma_start(
                        out=wo[:, wofs:wofs + NCCH * 128],
                        in_=wos.ap()[:, oc * NCCH * 128:(oc + 1) * NCCH * 128]),
                        waits=ph2 + wofree.get(slw, []))
                    wrdy = [(wosems[slw], ld)]
                    for th in range(NTH):
                        par = (oc * NTH + th) % 2
                        pb = cp[par]
                        for cc2 in range(NCCH):
                            ctx.op("T", lambda e, oc=oc, th=th, cc2=cc2, pb=pb, wofs=wofs:
                                   e.matmul(
                                pb[:, 0:512],
                                wo[:, wofs + cc2 * 128: wofs + cc2 * 128 + 128],
                                valsr[:, (cc2 * NTH + th) * 512:(cc2 * NTH + th + 1) * 512],
                                start=(cc2 == 0), stop=(cc2 == NCCH - 1)),
                                waits=(wrdy + [vlt[cc2 // (NCCH // HPC)]] +
                                       (cfree.get(par, []) if cc2 == 0 else [])),
                                tick=(cc2 == NCCH - 1))
                        tmm = ctx.cnt["T"]
                        if th == NTH - 1:
                            wofree[slw] = [("eT", tmm)]
                        pend.append((oc, th, tmm))
                        if len(pend) > 1:
                            op_epi(*pend.pop(0))
                for it in pend:
                    op_epi(*it)
                ctx.flush(blk)
            if stages >= 3:
                ffn_phase(1, residB,
                          {k: v for k, v in opm.items()
                           if isinstance(k, tuple) and isinstance(k[0], int)},
                          winit={i: [("dws%d" % i, ctx.dcnt["dws%d" % i])]
                                 for i in range(3)})
            ctx.dma("Y", "dout", lambda e: [e.dma_start(
                out=out_t.ap()[i * (D // 4):(i + 1) * (D // 4), :]
                .rearrange("(dc p) t -> p dc t", p=128),
                in_=residB[:, i * (DC // 4) * TC:(i + 1) * (DC // 4) * TC]
                .rearrange("p (dc t) -> p dc t", t=TC))
                for i in range(4)],
                waits=[("eV", ctx.cnt["V"]), ("eS", ctx.cnt["S"])], n=4)
            ctx.op("G", lambda e: e.nop(), waits=[("dout", ctx.dcnt["dout"])], tick=False)
            ctx.op("Y", lambda e: e.nop(), waits=[("dout", ctx.dcnt["dout"])], tick=False)
            ctx.flush(blk)

    return nc


def shard_inputs(cfg, L0_attn_x, L0_attn_w_qkv, L0_attn_b_qkv, L0_attn_w_o,
                 L0_attn_b_o, L0_ffn_weight, L0_ffn_bias, L0_ffn_gate_w,
                 L0_ffn_in_channel_scale, L0_ffn_out_channel_scale):
    D, TC, DH, HPC = cfg["D"], cfg["TC"], cfg["DH"], cfg["HPC"]
    PART = 4 * HPC * DH
    f32 = np.float32
    x = np.asarray(L0_attn_x, f32)
    wq = np.asarray(L0_attn_w_qkv, f32)
    bq = np.asarray(L0_attn_b_qkv, f32)
    wo = np.asarray(L0_attn_w_o, f32)
    bo = np.asarray(L0_attn_b_o, f32)
    wf = np.asarray(L0_ffn_weight, f32)
    bf = np.asarray(L0_ffn_bias, f32)
    gw = np.asarray(L0_ffn_gate_w, f32)
    icsv = np.asarray(L0_ffn_in_channel_scale, f32)
    ocsv = np.asarray(L0_ffn_out_channel_scale, f32)

    DC = D // 128
    NCCH = PART // 128
    NR = 4
    HPC_ = HPC
    perm = []
    for h in range(HPC_):
        for r in range(NR):
            base = r * HPC_ * DH + h * DH
            perm.extend(range(base, base + DH))
    woT = np.ascontiguousarray(wo.T[perm, :])
    wfT = np.ascontiguousarray(wf.T)
    wfs = np.ascontiguousarray(
        wfT.reshape(DC, 128, DC, 128).transpose(1, 2, 0, 3).reshape(128, DC * DC * 128))
    wos_a = np.ascontiguousarray(
        woT.reshape(NCCH, 128, DC, 128).transpose(1, 2, 0, 3).reshape(128, DC * NCCH * 128))
    com = dict(
        wfs=wfs, wos=wos_a,
        w_oT=np.ascontiguousarray(wo.T),
        b_o=np.ascontiguousarray(bo[None, :]),
        bo_pp=np.ascontiguousarray(bo.reshape(-1, 128).T),
        w_ffnT=np.ascontiguousarray(wf.T),
        b_ffn=np.ascontiguousarray(bf[None, :]),
        bf_pp=np.ascontiguousarray(bf.reshape(-1, 128).T),
        gateT=np.ascontiguousarray(gw.T),
        ics=np.ascontiguousarray(icsv),
        ocs=np.ascontiguousarray(ocsv),
    )
    in_maps = []
    for c in range(8):
        bidx, r = c // 4, c % 4
        xs = np.ascontiguousarray(x[bidx, r * TC:(r + 1) * TC, :].T)
        hs = [HPC * r + i for i in range(HPC)]
        qcols = np.concatenate([wq[h * DH:(h + 1) * DH, :].T for h in hs], axis=1)
        kcols = np.concatenate([wq[PART + h * DH: PART + (h + 1) * DH, :].T for h in hs], axis=1)
        vcols = np.concatenate([wq[2 * PART + h * DH: 2 * PART + (h + 1) * DH, :].T for h in hs], axis=1)
        wqkvT = np.ascontiguousarray(np.concatenate([qcols, kcols, vcols], axis=1))
        bq_sel = np.concatenate(
            [np.concatenate([bq[h * DH:(h + 1) * DH] for h in hs])] +
            [np.concatenate([bq[PART + h * DH: PART + (h + 1) * DH] for h in hs])] +
            [np.concatenate([bq[2 * PART + h * DH: 2 * PART + (h + 1) * DH] for h in hs])])
        qT = qcols.reshape(DC, 128, HPC, DH).transpose(1, 2, 0, 3)
        kT = kcols.reshape(DC, 128, HPC, DH).transpose(1, 2, 0, 3)
        vT = vcols.reshape(DC, 128, HPC, DH).transpose(1, 2, 0, 3)
        wqks = np.ascontiguousarray(
            np.concatenate([qT, kT], axis=3).reshape(128, HPC * DC * 2 * DH))
        wvs = np.ascontiguousarray(vT.reshape(128, HPC * DC * DH))
        m = dict(x_t=xs, wqkvT=wqkvT, bqkv=np.ascontiguousarray(bq_sel[None, :]),
                 bqkv_pp=np.ascontiguousarray(bq_sel.reshape(-1, 128).T),
                 wqks=wqks, wvs=wvs, **com)
        in_maps.append(m)
    return in_maps


def assemble(cfg, results):
    D, TC = cfg["D"], cfg["TC"]
    out = np.empty((2, 4 * TC, D), np.float32)
    for c in range(8):
        bidx, r = c // 4, c % 4
        out[bidx, r * TC:(r + 1) * TC, :] = results[c]["out_t"].T
    return out


def kernel(**inputs):
    cfg = FULL
    nc = build(cfg)
    in_maps = shard_inputs(cfg, **inputs)
    res = run_bass_kernel_spmd(nc, in_maps, core_ids=list(range(8)))
    return assemble(cfg, res.results)


# version 18
# speedup vs baseline: 1.1408x; 1.1408x over previous
import numpy as np
import ml_dtypes
from contextlib import ExitStack

import concourse.bass as bass
import concourse.mybir as mybir
from concourse.bass_utils import run_bass_kernel_spmd

F32 = mybir.dt.float32
BF16 = mybir.dt.bfloat16
AF = mybir.ActivationFunctionType
ALU = mybir.AluOpType
AX = mybir.AxisListType

MASK_NEG = -1.0e9
EPS = 1e-5
TH = 256

FULL = dict(D=4096, TC=1024, DH=256, HPC=2, E=8, stages=3)


class Ctx:
    def __init__(self, nc):
        self.nc = nc
        self.q = {k: [] for k in "TVSGY"}
        self.cnt = {k: 0 for k in "TVSGY"}
        self.sem = {}
        self.dcnt = {}
        self.waited = {k: {} for k in "TVSGY"}

    def mksem(self, stack, name):
        self.sem[name] = stack.enter_context(self.nc.semaphore(name))
        self.dcnt[name] = 0

    def _waits(self, e, waits):
        real = []
        for s, v in waits:
            if v and v > 0 and self.waited[e].get(s, -1) < v:
                real.append((s, v))
                self.waited[e][s] = v
        return real

    def op(self, e, fn, waits=(), tick=True):
        real = self._waits(e, waits)
        myc = None
        if tick:
            self.cnt[e] += 1
            myc = self.cnt[e]
        esem = "e" + e

        def emit(eng, real=real, fn=fn, tick=tick, esem=esem):
            for s, v in real:
                eng.wait_ge(self.sem[s], v)
            inst = fn(eng)
            if tick and inst is not None:
                inst.then_inc(self.sem[esem], 1)
        self.q[e].append(emit)
        return myc

    def dma(self, e, dsem, fn, waits=(), n=1):
        real = self._waits(e, waits)
        self.dcnt[dsem] += 16 * n
        tgt = self.dcnt[dsem]

        def emit(eng, real=real, fn=fn, dsem=dsem):
            for s, v in real:
                eng.wait_ge(self.sem[s], v)
            insts = fn(eng)
            if not isinstance(insts, (list, tuple)):
                insts = [insts]
            for i in insts:
                i.then_inc(self.sem[dsem], 16)
        self.q[e].append(emit)
        return tgt

    def cc(self, ins_ap, outs_ap, groups, waits=()):
        real = self._waits("G", waits)
        self.dcnt["cc"] += 1
        tgt = self.dcnt["cc"]

        def emit(eng, real=real):
            for s, v in real:
                eng.wait_ge(self.sem[s], v)
            eng.collective_compute("AllGather", ALU.bypass, replica_groups=groups,
                                   ins=[ins_ap], outs=[outs_ap]).then_inc(self.sem["cc"], 1)
        self.q["G"].append(emit)
        return tgt

    def barrier(self, engines="TVSGY", extra=()):
        marks = [("e" + e2, self.cnt[e2]) for e2 in "TVS"]
        marks += [(s, v) for s, v in self.dcnt.items() if v > 0 and s != "cc"]
        marks += list(extra)
        for e in engines:
            self.op(e, lambda eng: eng.nop(), waits=marks, tick=False)

    def flush(self, block):
        for e, attr in (("G", "gpsimd"), ("Y", "sync"), ("T", "tensor"),
                        ("V", "vector"), ("S", "scalar")):
            lst = self.q[e]
            if lst:
                self.q[e] = []
                getattr(block, attr)(lambda eng, lst=lst: [f(eng) for f in lst])


def _act_table_idx(nc):
    from concourse.hw_specs import get_activation_tables
    for idx, name in enumerate(get_activation_tables(nc.m.arch)):
        if name == "natural_log_exp_and_others":
            return idx
    raise ValueError("act table set not found")


def build(cfg):
    D, TC, DH, HPC, E = cfg["D"], cfg["TC"], cfg["DH"], cfg["HPC"], cfg["E"]
    stages = cfg["stages"]
    DC = D // 128
    SEQ = 4 * TC
    NTB = SEQ // 512
    NQ = TC // TH
    NT = TH // 128
    CC = DH // 128
    QKC = 2 * CC
    NVT = SEQ // 128
    NCCH = 4 * HPC * DH // 128
    NTH = TC // 512

    nc = bass.Bass("TRN2", target_bir_lowering=False)

    x_t = nc.dram_tensor("x_t", [D, TC], F32, kind="ExternalInput")
    wqkvT = nc.dram_tensor("wqkvT", [D, 3 * HPC * DH], F32, kind="ExternalInput")
    wqks = nc.dram_tensor("wqks", [128, HPC * DC * 2 * DH], F32, kind="ExternalInput")
    wvs = nc.dram_tensor("wvs", [128, HPC * DC * DH], F32, kind="ExternalInput")
    wfs = nc.dram_tensor("wfs", [128, DC * DC * 128], F32, kind="ExternalInput")
    wos = nc.dram_tensor("wos", [128, DC * (4 * HPC * DH)], F32, kind="ExternalInput")
    bqkv = nc.dram_tensor("bqkv", [1, 3 * HPC * DH], F32, kind="ExternalInput")
    bqkv_pp = nc.dram_tensor("bqkv_pp", [128, 3 * HPC * DH // 128], F32, kind="ExternalInput")
    bo_pp = nc.dram_tensor("bo_pp", [128, D // 128], F32, kind="ExternalInput")
    bf_pp = nc.dram_tensor("bf_pp", [128, D // 128], F32, kind="ExternalInput")
    w_oT = nc.dram_tensor("w_oT", [4 * HPC * DH, D], F32, kind="ExternalInput")
    b_o = nc.dram_tensor("b_o", [1, D], F32, kind="ExternalInput")
    w_ffnT = nc.dram_tensor("w_ffnT", [D, D], F32, kind="ExternalInput")
    b_ffn = nc.dram_tensor("b_ffn", [1, D], F32, kind="ExternalInput")
    gateT = nc.dram_tensor("gateT", [D, E], F32, kind="ExternalInput")
    ics = nc.dram_tensor("ics", [E, D], F32, kind="ExternalInput")
    ocs = nc.dram_tensor("ocs", [E, D], F32, kind="ExternalInput")
    out_t = nc.dram_tensor("out_t", [D, TC], F32, kind="ExternalOutput")

    spill = nc.dram_tensor("spill", [128, DC * TC], F32, kind="Internal")
    wsign = nc.dram_tensor("wsign", [128, DC * DC * 128], BF16, kind="Internal")
    NH2 = TC // 512
    y2inL = [nc.dram_tensor("y2in%d" % i, [D, 512], BF16, kind="Internal")
             for i in range(NH2)]
    y2outL = [nc.dram_tensor("y2out%d" % i, [4 * D, 512], BF16, kind="Internal")
              for i in range(NH2)]
    vinL = [nc.dram_tensor("vin%d" % h, [DH, SEQ], BF16, kind="Internal")
            for h in range(HPC)]
    voutL = [nc.dram_tensor("vout%d" % h, [4 * DH, SEQ], BF16, kind="Internal")
             for h in range(HPC)]

    id_f32 = nc.inline_tensor(np.eye(128, dtype=np.float32), name="id_f32")
    ones_r = nc.inline_tensor(np.ones((1, 128), dtype=np.float32), name="ones_r")
    ones_cb = nc.inline_tensor(np.ones((128, 1), dtype=ml_dtypes.bfloat16), name="ones_cb")
    i1 = np.arange(128)[:, None]
    i2 = np.arange(512)[None, :]
    tS = np.stack([np.where(i2 <= k * 128 + i1, 0.0, MASK_NEG) for k in range(4)])
    tT = np.stack([np.where(k * 128 + i1 <= i2, 0.0, MASK_NEG) for k in range(4)])
    triS_c = nc.inline_tensor(tS.reshape(512, 512).astype(ml_dtypes.bfloat16), name="triS")
    triT_c = nc.inline_tensor(tT.reshape(512, 512).astype(ml_dtypes.bfloat16), name="triT")

    groups = [[0, 1, 2, 3], [4, 5, 6, 7]]
    ctx = Ctx(nc)

    with ExitStack() as st:
        for s in ("eT", "eV", "eS", "eG", "eY",
                  "dx", "dw0", "dw1", "dw2", "dws", "dy2", "dsp",
                  "dwh0", "dwh1", "dsl0", "dsl1", "dvs", "dvl",
                  "dwo0", "dwo1", "dwo2", "drl", "dout", "cc", "dsm", "dmk",
                  "dsg", "dn0", "dn1", "dn2", "dws0", "dws1", "dws2", "dvl1",
                  "dy20", "dy21", "dy22", "dy23", "dvs0", "dvs1"):
            ctx.mksem(st, s)
        wsems = ["dw0", "dw1", "dw2"]
        nsems = ["dn0", "dn1", "dn2"]
        wssems = ["dws0", "dws1", "dws2"]
        slsems = ["dsl0", "dsl1"]
        whsems = ["dwh0", "dwh1"]
        wosems = ["dwo0", "dwo1", "dwo2"]

        p_ident = st.enter_context(nc.sbuf_tensor("p_ident", [128, 128], F32))
        p_ones_r = st.enter_context(nc.sbuf_tensor("p_ones_r", [1, 128], F32))
        p_ones_cb = st.enter_context(nc.sbuf_tensor("p_ones_cb", [128, 1], BF16))
        p_ics = st.enter_context(nc.sbuf_tensor("p_ics", [E, D], BF16))
        p_ocs = st.enter_context(nc.sbuf_tensor("p_ocs", [E, D], BF16))
        p_gate = st.enter_context(nc.sbuf_tensor("p_gate", [128, DC * E], BF16))
        p_gsb = st.enter_context(nc.sbuf_tensor("p_gsb", [128, E], F32))
        p_bq = st.enter_context(nc.sbuf_tensor("p_bq", [128, 3 * HPC * DH // 128], F32))
        p_bo = st.enter_context(nc.sbuf_tensor("p_bo", [128, DC], F32))
        p_hbf = st.enter_context(nc.sbuf_tensor("p_hbf", [128, DC], F32))
        p_sm = st.enter_context(nc.sbuf_tensor("p_sm", [128, 1120], F32))

        blk = st.enter_context(nc.Block())

        ctx.op("S", lambda e: e.add_instruction(mybir.InstLoadActFuncSet(
            name=nc.get_next_instruction_name(), act_func_set_id=_act_table_idx(nc),
            ins=[], outs=[])), tick=False)
        ctx.dma("Y", "dsm", lambda e: e.dma_start(out=p_ident[:, :], in_=id_f32.ap()))
        ctx.dma("Y", "dsm", lambda e: e.dma_start(out=p_ones_r[:, :], in_=ones_r.ap()))
        ctx.dma("Y", "dsm", lambda e: e.dma_start(out=p_ones_cb[:, :], in_=ones_cb.ap()))
        ctx.dma("G", "dsg", lambda e: e.dma_start(out=p_ics[:, :], in_=ics[:, :]))
        ctx.dma("G", "dsg", lambda e: e.dma_start(out=p_ocs[:, :], in_=ocs[:, :]))
        ctx.dma("G", "dsg", lambda e: e.dma_start(
            out=p_gate[:, :].rearrange("p (dc e) -> p dc e", e=E),
            in_=gateT.ap().rearrange("(dc p) e -> p dc e", p=128)))
        ctx.dma("G", "dsg", lambda e: e.dma_start(out=p_bq[:, :], in_=bqkv_pp.ap()))
        ctx.dma("G", "dsg", lambda e: e.dma_start(out=p_bo[:, :], in_=bo_pp.ap()))
        ctx.dma("G", "dsg", lambda e: e.dma_start(out=p_hbf[:, :], in_=bf_pp.ap()))
        sm_all = ctx.dcnt["dsm"]
        sg_all = ctx.dcnt["dsg"]
        cHB = ctx.op("S", lambda e: e.mul(out=p_hbf[:, :], in_=p_hbf[:, :], mul=0.5),
                     waits=[("dsg", sg_all)])

        with ExitStack() as ges:
            pg0 = ges.enter_context(nc.psum_tensor("pg0", [1, 512], F32))
            pg1 = ges.enter_context(nc.psum_tensor("pg1", [128, 512], F32))
            for dc in range(DC):
                ctx.op("T", lambda e, dc=dc: e.matmul(
                    pg0[0:1, 0:E], p_ones_cb[:, :], p_gate[:, dc * E:(dc + 1) * E],
                    start=(dc == 0), stop=(dc == DC - 1)),
                    waits=[("dsm", sm_all), ("dsg", sg_all)], tick=(dc == DC - 1))
            tgs = ctx.cnt["T"]
            cV = ctx.op("V", lambda e: e.tensor_copy(out=p_sm[0:1, 0:E], in_=pg0[0:1, 0:E]),
                        waits=[("eT", tgs)])
            tgb = ctx.op("T", lambda e: e.matmul(pg1[:, 0:E], p_ones_r[:, :],
                                                 p_sm[0:1, 0:E], start=True, stop=True),
                         waits=[("eV", cV)])
            cGS = ctx.op("V", lambda e: e.tensor_copy(out=p_gsb[:, :], in_=pg1[:, 0:E]),
                         waits=[("eT", tgb)])
            ctx.flush(blk)

        def ffn_phase(phase, rsd, resid_marks, winit=None, with_ln2=False):
            px = "f%d_" % phase
            out_marks = {}
            ctx.barrier()
            with ExitStack() as fes:
                xx = fes.enter_context(nc.sbuf_tensor(px + "xx", [128, 2 * 2 * TH], BF16))
                z2 = fes.enter_context(nc.sbuf_tensor(px + "z", [128, 2 * DC * TH], BF16))
                wsl = fes.enter_context(nc.sbuf_tensor(px + "w", [128, 2 * DC * 128], BF16))
                rb = fes.enter_context(nc.sbuf_tensor(px + "rb", [128, TH], F32))
                routT = fes.enter_context(nc.sbuf_tensor(px + "routT", [E, 2 * TH], BF16))
                ris = fes.enter_context(nc.sbuf_tensor(px + "ris", [128, TH], F32))
                t1 = fes.enter_context(nc.sbuf_tensor(px + "t1", [128, TH], F32))
                osc = fes.enter_context(nc.sbuf_tensor(px + "osc", [128, TH], BF16))
                t2 = fes.enter_context(nc.sbuf_tensor(px + "t2", [128, TH], F32))
                y2b = (fes.enter_context(nc.sbuf_tensor(px + "y2b", [128, 4 * TH], BF16))
                       if with_ln2 else None)
                pst = fes.enter_context(nc.psum_tensor(px + "pst", [128, 512], F32))
                pl0 = fes.enter_context(nc.psum_tensor(px + "pl0", [128, 512], F32))
                pl1 = fes.enter_context(nc.psum_tensor(px + "pl1", [128, 512], F32))
                pis0 = fes.enter_context(nc.psum_tensor(px + "pis0", [128, 512], F32))
                pis1 = fes.enter_context(nc.psum_tensor(px + "pis1", [128, 512], F32))
                pm0 = fes.enter_context(nc.psum_tensor(px + "pm0", [128, 512], F32))
                pm1 = fes.enter_context(nc.psum_tensor(px + "pm1", [128, 512], F32))
                pos = fes.enter_context(nc.psum_tensor(px + "pos", [128, 512], F32))
                plog = [pl0, pl1]
                pis = [pis0, pis1]
                pm = [pm0, pm1]
                wfree = dict(winit or {})
                wstores = {}
                t1last = []
                xxfree = {}
                pmfree = {}
                posfree = []
                t2free = {}
                oscfree = []
                risfree = []
                t1free = []
                prev_c6 = 0
                pstfree = []
                plfree = []
                pisfree = {0: [], 1: []}
                routTfree = {0: [], 1: []}
                zstate = {}

                def emit_pre(q):
                    nonlocal prev_c6
                    b = 0
                    negmean = p_sm[0:1, b + 0:b + TH]
                    rstd = p_sm[0:1, b + 256:b + 256 + TH]
                    ex2 = p_sm[0:1, 512:512 + TH]
                    m2 = p_sm[0:1, 768:768 + TH]
                    rb2 = 1024
                    for dc in range(DC):
                        sl = dc % 2
                        xo = sl * 2 * TH
                        w = list(resid_marks.get((dc, q), [])) + xxfree.get(sl, [])
                        cs1 = ctx.op("S", lambda e, dc=dc, xo=xo, q=q: e.activation(
                            out=xx[:, xo:xo + TH],
                            in_=rsd[:, dc * TC + q * TH: dc * TC + q * TH + TH],
                            func=AF.Copy), waits=w)
                        cs2 = ctx.op("S", lambda e, dc=dc, xo=xo, q=q: e.activation(
                            out=xx[:, xo + TH:xo + 2 * TH],
                            in_=rsd[:, dc * TC + q * TH: dc * TC + q * TH + TH],
                            func=AF.Square))
                        ctx.op("T", lambda e, dc=dc, xo=xo: e.matmul(
                            pst[0:1, 0:2 * TH], p_ones_cb[:, :],
                            xx[:, xo:xo + 2 * TH],
                            start=(dc == 0), stop=(dc == DC - 1)),
                            waits=[("eS", cs2)] + (pstfree if dc == 0 else []),
                            tick=False)
                        for tt in range(NT):
                            tdc = ctx.op("T", lambda e, dc=dc, xo=xo, tt=tt: e.matmul(
                                plog[tt][:, 0:E],
                                xx[:, xo + tt * 128: xo + tt * 128 + 128],
                                p_gate[:, dc * E:(dc + 1) * E],
                                start=(dc == 0), stop=(dc == DC - 1)),
                                waits=(plfree if (dc == 0 and tt == 0) else []),
                                tick=(tt == NT - 1))
                        xxfree[sl] = [("eT", tdc)]
                    tsums = ctx.cnt["T"]
                    c1 = ctx.op("V", lambda e, negmean=negmean: e.tensor_scalar(
                        out=negmean, in0=pst[0:1, 0:TH], scalar1=-1.0 / D, scalar2=None,
                        op0=ALU.mult), waits=[("eT", tsums), ("eS", prev_c6)])
                    c2 = ctx.op("V", lambda e, ex2=ex2: e.tensor_scalar(
                        out=ex2, in0=pst[0:1, TH:2 * TH], scalar1=1.0 / D, scalar2=None,
                        op0=ALU.mult), waits=[("eS", prev_c6)])
                    c3 = ctx.op("V", lambda e, negmean=negmean, m2=m2: e.tensor_mul(
                        out=m2, in0=negmean, in1=negmean), waits=[("eV", c1)])
                    c4 = ctx.op("V", lambda e, ex2=ex2, m2=m2: e.tensor_sub(
                        out=ex2, in0=ex2, in1=m2), waits=[("eV", c3), ("eV", c2)])

                    c4b = ctx.op("V", lambda e, ex2=ex2: e.tensor_scalar_add(
                        out=ex2, in0=ex2, scalar1=EPS), waits=[("eV", c4)])
                    c5 = ctx.op("S", lambda e, ex2=ex2, m2=m2: e.activation(
                        out=m2, in_=ex2, func=AF.Ln), waits=[("eV", c4b)])
                    c6 = ctx.op("S", lambda e, m2=m2, rstd=rstd: e.activation(
                        out=rstd, in_=m2, func=AF.Exp, scale=-0.5), waits=[("eS", c5)])
                    prev_c6 = c6
                    tb1 = ctx.op("T", lambda e, negmean=negmean: e.matmul(
                        pst[:, 0:TH], p_ones_r[:, :], negmean, start=True, stop=True),
                        waits=[("eV", c1), ("eV", c2)] + pstfree)
                    tb2 = ctx.op("T", lambda e, rstd=rstd: e.matmul(
                        pis0[:, TH:2 * TH], p_ones_r[:, :], rstd, start=True, stop=True),
                        waits=[("eS", c6)] + pisfree[0])
                    vb2 = ctx.op("V", lambda e: e.tensor_copy(out=rb[:, :], in_=pis0[:, TH:2 * TH]),
                                 waits=[("eT", tb2)])
                    tcl = []
                    for tt in range(NT):
                        tcl.append(ctx.op("T", lambda e, tt=tt, negmean=negmean: e.transpose(
                            pis1[:, tt:tt + 1], negmean[0:1, tt * 128:(tt + 1) * 128],
                            p_ident[0:1, 0:1]),
                            waits=[("eV", c1), ("eS", c6)] + (pisfree[1] if tt == 0 else [])))
                        tcl.append(ctx.op("T", lambda e, tt=tt, rstd=rstd: e.transpose(
                            pis1[:, NT + tt:NT + tt + 1], rstd[0:1, tt * 128:(tt + 1) * 128],
                            p_ident[0:1, 0:1])))
                    vcl = ctx.op("V", lambda e, rb2=rb2: e.tensor_copy(
                        out=p_sm[:, rb2:rb2 + 2 * NT], in_=pis1[:, 0:2 * NT]),
                        waits=[("eT", tcl[-1])])
                    rT_done = []
                    prev_tr = 0
                    for tt in range(NT):
                        l0 = rb2 + 8
                        cr1 = ctx.op("V", lambda e, tt=tt, rb2=rb2, l0=l0: e.scalar_tensor_tensor(
                            out=p_sm[:, l0:l0 + E], in0=p_gsb[:, :],
                            scalar=p_sm[:, rb2 + tt:rb2 + tt + 1],
                            in1=plog[tt][:, 0:E], op0=ALU.mult, op1=ALU.add),
                            waits=[("eV", vcl), ("eV", cGS), ("eT", tsums),
                                   ("eT", prev_tr)])
                        cr2 = ctx.op("V", lambda e, tt=tt, rb2=rb2, l0=l0: e.tensor_scalar(
                            out=p_sm[:, l0:l0 + E], in0=p_sm[:, l0:l0 + E],
                            scalar1=p_sm[:, rb2 + NT + tt:rb2 + NT + tt + 1], scalar2=None,
                            op0=ALU.mult), waits=[("eV", cr1)])
                        cr3 = ctx.op("V", lambda e, rb2=rb2, l0=l0: e.reduce_max(
                            out=p_sm[:, rb2 + 4:rb2 + 5], in_=p_sm[:, l0:l0 + E],
                            axis=AX.X, negate=True), waits=[("eV", cr2)])
                        cr4 = ctx.op("S", lambda e, rb2=rb2, l0=l0: e.activation(
                            out=p_sm[:, l0 + 8:l0 + 8 + E], in_=p_sm[:, l0:l0 + E],
                            func=AF.Exp, bias=p_sm[:, rb2 + 4:rb2 + 5]),
                            waits=[("eV", cr3)])
                        cr5 = ctx.op("V", lambda e, rb2=rb2, l0=l0: e.reduce_sum(
                            out=p_sm[:, rb2 + 5:rb2 + 6], in_=p_sm[:, l0 + 8:l0 + 8 + E],
                            axis=AX.X), waits=[("eS", cr4)])
                        cr6 = ctx.op("V", lambda e, rb2=rb2: e.reciprocal(
                            out=p_sm[:, rb2 + 6:rb2 + 7], in_=p_sm[:, rb2 + 5:rb2 + 6]),
                            waits=[("eV", cr5)])
                        cr7 = ctx.op("V", lambda e, rb2=rb2, l0=l0: e.tensor_scalar(
                            out=p_sm[:, l0 + 8:l0 + 8 + E], in0=p_sm[:, l0 + 8:l0 + 8 + E],
                            scalar1=p_sm[:, rb2 + 6:rb2 + 7], scalar2=None, op0=ALU.mult),
                            waits=[("eV", cr6)])
                        prev_tr = ctx.op("T", lambda e, tt=tt, l0=l0: e.transpose(
                            pis1[0:E, 2 * NT + tt * 128: 2 * NT + (tt + 1) * 128],
                            p_sm[:, l0 + 8:l0 + 8 + E], p_ident[:, :]),
                            waits=[("eV", cr7)])
                        rT_done.append(prev_tr)
                    rofs = (q % 2) * TH
                    vrt = ctx.op("V", lambda e, rofs=rofs: e.tensor_copy(
                        out=routT[0:E, rofs:rofs + TH], in_=pis1[0:E, 2 * NT:2 * NT + TH]),
                        waits=[("eT", rT_done[-1])] + routTfree[q % 2])
                    pisfree[1] = [("eV", vrt)]
                    zofs = (q % 2) * DC * TH
                    zml = []
                    zwar = zstate.get(q - 2, (None, []))[1]
                    for dc in range(DC):
                        sl = dc % 2
                        pb = pis[sl]
                        ti = ctx.op("T", lambda e, dc=dc, pb=pb, rofs=rofs: e.matmul(
                            pb[:, 0:TH], p_ics[0:E, dc * 128:(dc + 1) * 128],
                            routT[0:E, rofs:rofs + TH], start=True, stop=True),
                            waits=[("eV", vrt)] +
                            ([("eV", vb2)] if dc == 0 else []) + t1last[:])
                        v1 = ctx.op("V", lambda e, pb=pb: e.tensor_mul(
                            out=ris[:, 0:TH], in0=pb[:, 0:TH], in1=rb[:, :]),
                            waits=[("eT", ti), ("eV", vb2)] + t1last[:])
                        v2 = ctx.op("V", lambda e, dc=dc, q=q: e.tensor_add(
                            out=t1[:, 0:TH],
                            in0=rsd[:, dc * TC + q * TH: dc * TC + q * TH + TH],
                            in1=pst[:, 0:TH]),
                            waits=[("eT", tb1), ("eV", v1)] + t1last[:] +
                            list(resid_marks.get((dc, q), [])))
                        v3 = ctx.op("V", lambda e, dc=dc, zofs=zofs: e.tensor_mul(
                            out=z2[:, zofs + dc * TH: zofs + (dc + 1) * TH],
                            in0=t1[:, 0:TH], in1=ris[:, 0:TH]),
                            waits=[("eV", v2)] + zwar)
                        t1last[:] = [("eV", v3)]
                        zml.append([("eV", v3)])
                    pisfree[0] = [("eV", ctx.cnt["V"])]
                    pisfree[1] = [("eV", ctx.cnt["V"])]
                    plfree[:] = [("eV", ctx.cnt["V"])]
                    pstfree[:] = [("eV", ctx.cnt["V"])]
                    return zml, vrt

                def emit_main(q, zml, vrt):
                    zofs = (q % 2) * DC * TH
                    rofs = (q % 2) * TH
                    tmm = 0
                    tos = 0
                    for oc in range(DC):
                        slw = oc % 2
                        wofs = slw * DC * 128
                        if phase == 0 and q == 0:
                            ld = ctx.dma("G", wsems[slw], lambda e, oc=oc, wofs=wofs:
                                         e.dma_start(
                                out=wsl[:, wofs:wofs + DC * 128],
                                in_=wfs.ap()[:, oc * DC * 128:(oc + 1) * DC * 128]),
                                waits=wfree.get(slw, []))
                            sg = ctx.op("S", lambda e, wofs=wofs: e.activation(
                                out=wsl[:, wofs:wofs + DC * 128],
                                in_=wsl[:, wofs:wofs + DC * 128], func=AF.Sign),
                                waits=[(wsems[slw], ld)])
                            stw = ctx.dma("Y", wssems[slw], lambda e, oc=oc, wofs=wofs:
                                          e.dma_start(
                                out=wsign.ap()[:, oc * DC * 128:(oc + 1) * DC * 128],
                                in_=wsl[:, wofs:wofs + DC * 128]), waits=[("eS", sg)])
                            wrdy = [("eS", sg)]
                            wfree[slw] = [(wssems[slw], stw)]
                            wstores[oc] = (wssems[slw], stw)
                        else:
                            wts = [wstores[oc]] if (phase == 0 and oc in wstores) else []
                            ld = ctx.dma("Y", nsems[slw], lambda e, oc=oc, wofs=wofs:
                                         e.dma_start(
                                out=wsl[:, wofs:wofs + DC * 128],
                                in_=wsign.ap()[:, oc * DC * 128:(oc + 1) * DC * 128]),
                                waits=wfree.get(slw, []) + wts)
                            wrdy = [(nsems[slw], ld)]
                            wfree[slw] = []
                        pb = pm[oc % 2]
                        for dc in range(DC):
                            ctx.op("T", lambda e, oc=oc, dc=dc, pb=pb, wofs=wofs, zofs=zofs:
                                   e.matmul(
                                pb[:, 0:TH], wsl[:, wofs + dc * 128:wofs + dc * 128 + 128],
                                z2[:, zofs + dc * TH: zofs + (dc + 1) * TH],
                                start=(dc == 0), stop=(dc == DC - 1)),
                                waits=(wrdy + zml[dc] +
                                       (pmfree.get(oc % 2, []) if dc == 0 else [])),
                                tick=(dc == DC - 1))
                        tmm = ctx.cnt["T"]
                        if phase == 1:
                            wfree[slw] = [("eT", tmm)]
                        else:
                            wfree[slw] = wfree[slw] + [("eT", tmm)]
                        tos = ctx.op("T", lambda e, oc=oc, rofs=rofs: e.matmul(
                            pos[:, 0:TH], p_ocs[0:E, oc * 128:(oc + 1) * 128],
                            routT[0:E, rofs:rofs + TH], start=True, stop=True),
                            waits=[("eV", vrt)] + posfree)
                        so = ctx.op("S", lambda e, oc=oc: e.activation(
                            out=osc[:, :], in_=pos[:, 0:TH],
                            func=AF.Copy), waits=[("eT", tos)] + oscfree)
                        posfree[:] = [("eS", so)]
                        v4 = ctx.op("V", lambda e, oc=oc, pb=pb: e.tensor_mul(
                            out=t2[:, :], in0=pb[:, 0:TH],
                            in1=osc[:, :]),
                            waits=[("eT", tmm), ("eS", so)] + t2free.get(0, []))
                        pmfree[oc % 2] = [("eV", v4)]
                        oscfree[:] = [("eV", v4)]
                        s5 = ctx.op("V", lambda e, oc=oc: e.tensor_scalar(
                            out=t2[:, :], in0=t2[:, :],
                            scalar1=0.5, scalar2=p_hbf[:, oc:oc + 1],
                            op0=ALU.mult, op1=ALU.add),
                            waits=[("eV", v4), ("eS", cHB)])
                        v6 = ctx.op("V", lambda e, oc=oc, q=q: e.tensor_add(
                            out=rsd[:, oc * TC + q * TH: oc * TC + q * TH + TH],
                            in0=rsd[:, oc * TC + q * TH: oc * TC + q * TH + TH],
                            in1=t2[:, :]),
                            waits=[("eV", s5)])
                        t2free[0] = [("eV", v6)]
                        out_marks[(oc, q)] = [("eV", v6)]
                    routTfree[q % 2] = [("eT", tos)]
                    return tmm

                y2free = {}
                ln2state = {"c6": 0}

                def emit_ln2q(q):
                    negmean = p_sm[0:1, 0:TH]
                    rstd = p_sm[0:1, 256:256 + TH]
                    ex2 = p_sm[0:1, 512:512 + TH]
                    m2 = p_sm[0:1, 768:768 + TH]
                    for dc in range(DC):
                        sl = dc % 2
                        xo = sl * 2 * TH
                        w = list(out_marks.get((dc, q), [])) + xxfree.get(sl, [])
                        cs1 = ctx.op("S", lambda e, dc=dc, xo=xo, q=q: e.activation(
                            out=xx[:, xo:xo + TH],
                            in_=rsd[:, dc * TC + q * TH: dc * TC + q * TH + TH],
                            func=AF.Copy), waits=w)
                        cs2 = ctx.op("S", lambda e, dc=dc, xo=xo, q=q: e.activation(
                            out=xx[:, xo + TH:xo + 2 * TH],
                            in_=rsd[:, dc * TC + q * TH: dc * TC + q * TH + TH],
                            func=AF.Square))
                        tdc = ctx.op("T", lambda e, dc=dc, xo=xo: e.matmul(
                            pis1[0:1, 0:2 * TH], p_ones_cb[:, :],
                            xx[:, xo:xo + 2 * TH],
                            start=(dc == 0), stop=(dc == DC - 1)),
                            waits=[("eS", cs2)] + (pisfree[1] if dc == 0 else []),
                            tick=True)
                        xxfree[sl] = [("eT", tdc)]
                    tsums = ctx.cnt["T"]
                    pc6 = ln2state["c6"]
                    c1 = ctx.op("V", lambda e, negmean=negmean: e.tensor_scalar(
                        out=negmean, in0=pis1[0:1, 0:TH], scalar1=-1.0 / D, scalar2=None,
                        op0=ALU.mult), waits=[("eT", tsums), ("eS", pc6)])
                    c2 = ctx.op("V", lambda e, ex2=ex2: e.tensor_scalar(
                        out=ex2, in0=pis1[0:1, TH:2 * TH], scalar1=1.0 / D, scalar2=None,
                        op0=ALU.mult), waits=[("eS", pc6)])
                    c3 = ctx.op("V", lambda e, negmean=negmean, m2=m2: e.tensor_mul(
                        out=m2, in0=negmean, in1=negmean), waits=[("eV", c1)])
                    c4 = ctx.op("V", lambda e, ex2=ex2, m2=m2: e.tensor_sub(
                        out=ex2, in0=ex2, in1=m2), waits=[("eV", c3), ("eV", c2)])
                    c4b = ctx.op("V", lambda e, ex2=ex2: e.tensor_scalar_add(
                        out=ex2, in0=ex2, scalar1=EPS), waits=[("eV", c4)])
                    c5 = ctx.op("S", lambda e, ex2=ex2, m2=m2: e.activation(
                        out=m2, in_=ex2, func=AF.Ln), waits=[("eV", c4b)])
                    c6 = ctx.op("S", lambda e, m2=m2, rstd=rstd: e.activation(
                        out=rstd, in_=m2, func=AF.Exp, scale=-0.5), waits=[("eS", c5)])
                    ln2state["c6"] = c6
                    tb1L = ctx.op("T", lambda e, negmean=negmean: e.matmul(
                        pis1[:, 0:TH], p_ones_r[:, :], negmean, start=True, stop=True),
                        waits=[("eV", c1), ("eV", c2)])
                    tb2L = ctx.op("T", lambda e, rstd=rstd: e.matmul(
                        pis1[:, TH:2 * TH], p_ones_r[:, :], rstd, start=True, stop=True),
                        waits=[("eS", c6)])
                    for dc in range(DC):
                        sl4 = dc % 4
                        v2 = ctx.op("V", lambda e, dc=dc, q=q: e.tensor_add(
                            out=t1[:, 0:TH],
                            in0=rsd[:, dc * TC + q * TH: dc * TC + q * TH + TH],
                            in1=pis1[:, 0:TH]),
                            waits=[("eT", tb1L)] + t1last[:] +
                            list(out_marks.get((dc, q), [])))
                        v3 = ctx.op("V", lambda e, dc=dc, sl4=sl4: e.tensor_mul(
                            out=y2b[:, sl4 * TH:(sl4 + 1) * TH],
                            in0=t1[:, 0:TH], in1=pis1[:, TH:2 * TH]),
                            waits=[("eT", tb2L), ("eV", v2)] + y2free.get(sl4, []))
                        t1last[:] = [("eV", v3)]
                        dd = ctx.dma("Y", "dy2%d" % sl4, lambda e, dc=dc, q=q, sl4=sl4:
                                     e.dma_start(
                            out=y2inL[q // 2].ap()[dc * 128:(dc + 1) * 128,
                                                   (q % 2) * TH:(q % 2) * TH + TH],
                            in_=y2b[:, sl4 * TH:(sl4 + 1) * TH]), waits=[("eV", v3)])
                        y2free[sl4] = [("dy2%d" % sl4, dd)]
                        if q == NQ - 1:
                            ctx.dma("Y", "dsp", lambda e, dc=dc: e.dma_start(
                                out=spill.ap()[:, dc * TC:(dc + 1) * TC],
                                in_=rsd[:, dc * TC:(dc + 1) * TC]), waits=[("eV", v3)])
                    pisfree[1] = [("eV", ctx.cnt["V"])]
                    if q % 2 == 1:
                        h2 = q // 2
                        ctx.cc(y2inL[h2].ap().opt(), y2outL[h2].ap().opt(), groups,
                               waits=[("dy2%d" % i, ctx.dcnt["dy2%d" % i])
                                      for i in range(4)])
                        cct.append(ctx.dcnt["cc"])

                for q in range(NQ):
                    zml, vrt = emit_pre(q)
                    zstate[q] = (zml, vrt)
                    if q >= 1:
                        pz, pv = zstate[q - 1][0], zstate[q - 1][1]
                        tmm = emit_main(q - 1, pz, pv)
                        zstate[q - 1] = (pz, [("eT", tmm)])
                        if with_ln2:
                            emit_ln2q(q - 1)
                tmm = emit_main(NQ - 1, zstate[NQ - 1][0], zstate[NQ - 1][1])
                zstate[NQ - 1] = (zstate[NQ - 1][0], [("eT", tmm)])
                if with_ln2:
                    emit_ln2q(NQ - 1)
                ctx.flush(blk)
            return out_marks

        cct = []

        def ln2_phase(rsd, resid_marks):
            ctx.barrier()
            with ExitStack() as les:
                xb = les.enter_context(nc.sbuf_tensor("l_xb", [128, 2 * TH], BF16))
                xsq = les.enter_context(nc.sbuf_tensor("l_xsq", [128, 2 * TH], BF16))
                nmb = les.enter_context(nc.sbuf_tensor("l_nmb", [128, TH], F32))
                rb = les.enter_context(nc.sbuf_tensor("l_rb", [128, TH], F32))
                y2b = les.enter_context(nc.sbuf_tensor("l_y2", [128, 4 * TH], BF16))
                t1 = les.enter_context(nc.sbuf_tensor("l_t1", [128, 2 * TH], F32))
                pSx = les.enter_context(nc.psum_tensor("l_pSx", [1, 512], F32))
                pSx2 = les.enter_context(nc.psum_tensor("l_pSx2", [1, 512], F32))
                pB = les.enter_context(nc.psum_tensor("l_pB", [128, 512], F32))
                prev_c6 = 0
                xbfree = {}
                t1free = {}
                y2free = {}
                for q in range(NQ):
                    b = 0
                    negmean = p_sm[0:1, b + 0:b + TH]
                    rstd = p_sm[0:1, b + 256:b + 256 + TH]
                    ex2 = p_sm[0:1, 512:512 + TH]
                    m2 = p_sm[0:1, 768:768 + TH]
                    for dc in range(DC):
                        sl = dc % 2
                        w = list(resid_marks.get((dc, q), [])) + xbfree.get(sl, [])
                        cs1 = ctx.op("S", lambda e, dc=dc, sl=sl, q=q: e.activation(
                            out=xb[:, sl * TH:(sl + 1) * TH],
                            in_=rsd[:, dc * TC + q * TH: dc * TC + q * TH + TH],
                            func=AF.Copy), waits=w)
                        cs2 = ctx.op("S", lambda e, dc=dc, sl=sl, q=q: e.activation(
                            out=xsq[:, sl * TH:(sl + 1) * TH],
                            in_=rsd[:, dc * TC + q * TH: dc * TC + q * TH + TH],
                            func=AF.Square))
                        ctx.op("T", lambda e, dc=dc, sl=sl: e.matmul(
                            pSx[0:1, 0:TH], p_ones_cb[:, :], xb[:, sl * TH:(sl + 1) * TH],
                            start=(dc == 0), stop=(dc == DC - 1)),
                            waits=[("eS", cs1)], tick=False)
                        tdc = ctx.op("T", lambda e, dc=dc, sl=sl: e.matmul(
                            pSx2[0:1, 0:TH], p_ones_cb[:, :], xsq[:, sl * TH:(sl + 1) * TH],
                            start=(dc == 0), stop=(dc == DC - 1)),
                            waits=[("eS", cs2)], tick=True)
                        xbfree[sl] = [("eT", tdc)]
                    tsums = ctx.cnt["T"]
                    c1 = ctx.op("V", lambda e, negmean=negmean: e.tensor_scalar(
                        out=negmean, in0=pSx[0:1, 0:TH], scalar1=-1.0 / D, scalar2=None,
                        op0=ALU.mult), waits=[("eT", tsums), ("eS", prev_c6)])
                    c2 = ctx.op("V", lambda e, ex2=ex2: e.tensor_scalar(
                        out=ex2, in0=pSx2[0:1, 0:TH], scalar1=1.0 / D, scalar2=None,
                        op0=ALU.mult), waits=[("eS", prev_c6)])
                    c3 = ctx.op("V", lambda e, negmean=negmean, m2=m2: e.tensor_mul(
                        out=m2, in0=negmean, in1=negmean), waits=[("eV", c1)])
                    c4 = ctx.op("V", lambda e, ex2=ex2, m2=m2: e.tensor_sub(
                        out=ex2, in0=ex2, in1=m2), waits=[("eV", c3), ("eV", c2)])
                    c4b = ctx.op("V", lambda e, ex2=ex2: e.tensor_scalar_add(
                        out=ex2, in0=ex2, scalar1=EPS), waits=[("eV", c4)])
                    c5 = ctx.op("S", lambda e, ex2=ex2, m2=m2: e.activation(
                        out=m2, in_=ex2, func=AF.Ln), waits=[("eV", c4b)])
                    c6 = ctx.op("S", lambda e, m2=m2, rstd=rstd: e.activation(
                        out=rstd, in_=m2, func=AF.Exp, scale=-0.5), waits=[("eS", c5)])
                    prev_c6 = c6
                    tb1 = ctx.op("T", lambda e, negmean=negmean: e.matmul(
                        pB[:, 0:TH], p_ones_r[:, :], negmean, start=True, stop=True),
                        waits=[("eV", c1)])
                    vb1 = ctx.op("V", lambda e: e.tensor_copy(out=nmb[:, :], in_=pB[:, 0:TH]),
                                 waits=[("eT", tb1)])
                    tb2 = ctx.op("T", lambda e, rstd=rstd: e.matmul(
                        pB[:, TH:2 * TH], p_ones_r[:, :], rstd, start=True, stop=True),
                        waits=[("eS", c6)])
                    vb2 = ctx.op("V", lambda e: e.tensor_copy(out=rb[:, :], in_=pB[:, TH:2 * TH]),
                                 waits=[("eT", tb2)])
                    for dc in range(DC):
                        sl = dc % 2
                        sl4 = dc % 4
                        v2 = ctx.op("V", lambda e, dc=dc, sl=sl, q=q: e.tensor_add(
                            out=t1[:, sl * TH:(sl + 1) * TH],
                            in0=rsd[:, dc * TC + q * TH: dc * TC + q * TH + TH],
                            in1=nmb[:, :]),
                            waits=[("eV", vb1)] + t1free.get(sl, []) +
                            list(resid_marks.get((dc, q), [])))
                        v3 = ctx.op("V", lambda e, dc=dc, sl=sl, sl4=sl4: e.tensor_mul(
                            out=y2b[:, sl4 * TH:(sl4 + 1) * TH],
                            in0=t1[:, sl * TH:(sl + 1) * TH], in1=rb[:, :]),
                            waits=[("eV", v2)] + y2free.get(sl4, []) + [("eV", vb2)])
                        t1free[sl] = [("eV", v3)]
                        dd = ctx.dma("Y", "dy2%d" % sl4, lambda e, dc=dc, q=q, sl4=sl4:
                                     e.dma_start(
                            out=y2inL[q // 2].ap()[dc * 128:(dc + 1) * 128,
                                                   (q % 2) * TH:(q % 2) * TH + TH],
                            in_=y2b[:, sl4 * TH:(sl4 + 1) * TH]), waits=[("eV", v3)])
                        y2free[sl4] = [("dy2%d" % sl4, dd)]
                        if q == NQ - 1:
                            ctx.dma("Y", "dsp", lambda e, dc=dc: e.dma_start(
                                out=spill.ap()[:, dc * TC:(dc + 1) * TC],
                                in_=rsd[:, dc * TC:(dc + 1) * TC]), waits=[("eV", v3)])
                    if q % 2 == 1:
                        h2 = q // 2
                        ctx.cc(y2inL[h2].ap().opt(), y2outL[h2].ap().opt(), groups,
                               waits=[("dy2%d" % i, ctx.dcnt["dy2%d" % i])
                                      for i in range(4)])
                        cct.append(ctx.dcnt["cc"])
                ctx.flush(blk)

        with nc.sbuf_tensor("residA", [128, DC * TC], F32) as residA:
            ctx.dma("Y", "dx", lambda e: [e.dma_start(
                out=residA[:, i * (DC // 4) * TC:(i + 1) * (DC // 4) * TC]
                .rearrange("p (dc t) -> p dc t", t=TC),
                in_=x_t.ap()[i * (D // 4):(i + 1) * (D // 4), :]
                .rearrange("(dc p) t -> p dc t", p=128)) for i in range(4)], n=4)
            xin = ctx.dcnt["dx"]
            xmarks = {(dc, q): [("dx", xin)] for dc in range(DC) for q in range(NQ)}
            ctx.flush(blk)
            if stages >= 1:
                rm = ffn_phase(0, residA, xmarks, with_ln2=(stages >= 2))
            else:
                rm = xmarks
            if stages == 1:
                ctx.dma("Y", "dout", lambda e: [e.dma_start(
                    out=out_t.ap()[i * (D // 4):(i + 1) * (D // 4), :]
                    .rearrange("(dc p) t -> p dc t", p=128),
                    in_=residA[:, i * (DC // 4) * TC:(i + 1) * (DC // 4) * TC]
                    .rearrange("p (dc t) -> p dc t", t=TC))
                    for i in range(4)],
                    waits=[("eV", ctx.cnt["V"]), ("eS", ctx.cnt["S"])], n=4)
                ctx.op("G", lambda e: e.nop(), waits=[("dout", ctx.dcnt["dout"])], tick=False)
                ctx.flush(blk)
                return nc
            ctx.flush(blk)
        aT, aV, aS = ctx.cnt["T"], ctx.cnt["V"], ctx.cnt["S"]
        spdone = ctx.dcnt["dsp"]
        ph_gate = ([("dsp", spdone), ("eT", aT), ("eV", aV), ("eS", aS)] +
                   [("dy2%d" % i, ctx.dcnt["dy2%d" % i]) for i in range(4)])

        am = {}
        ccvt = []
        ctx.barrier()
        with ExitStack() as aes:
            wqk = aes.enter_context(nc.sbuf_tensor("a_wqk", [128, DC * 2 * DH], BF16))
            wv = aes.enter_context(nc.sbuf_tensor("a_wv", [128, DC * DH], BF16))
            Kh = aes.enter_context(nc.sbuf_tensor("a_K", [128, CC * SEQ], BF16))
            Vh = aes.enter_context(nc.sbuf_tensor("a_V", [128, NVT * DH], BF16))
            Qc = aes.enter_context(nc.sbuf_tensor("a_Q", [128, 2 * CC * 512], BF16))
            y2b2 = aes.enter_context(nc.sbuf_tensor("a_y2b", [128, 2 * DC * 512], BF16))
            triS = aes.enter_context(nc.sbuf_tensor("a_triS", [128, 4 * 512], BF16))
            triT = aes.enter_context(nc.sbuf_tensor("a_triT", [128, 4 * 512], BF16))
            stmp = aes.enter_context(nc.sbuf_tensor("a_stmp", [128, 3 * 512], F32))
            ptb = aes.enter_context(nc.sbuf_tensor("a_pt", [128, 4 * 512], BF16))
            nmb2 = aes.enter_context(nc.sbuf_tensor("a_nmb", [128, 512], F32))
            rcb = aes.enter_context(nc.sbuf_tensor("a_rcb", [128, 512], F32))
            bvb = aes.enter_context(nc.sbuf_tensor("a_bvb", [128, DH], F32))
            vo = aes.enter_context(nc.sbuf_tensor("a_vo", [128, 2 * 512], BF16))
            p_bqf = aes.enter_context(nc.sbuf_tensor("a_bqf", [1, 3 * HPC * DH], F32))
            PS = aes.enter_context(nc.psum_tensor("a_p", [128, 8 * 512], F32))

            def bank(bi, w=512):
                return PS[:, bi * 512: bi * 512 + w]
            ctx.dma("Y", "dmk", lambda e: e.dma_start(
                out=triS[:, :].rearrange("p (k f) -> p k f", f=512),
                in_=triS_c.ap().rearrange("(k p) f -> p k f", p=128)),
                waits=ph_gate)
            ctx.dma("Y", "dmk", lambda e: e.dma_start(out=p_bqf[:, :], in_=bqkv.ap()))
            ctx.dma("Y", "dmk", lambda e: e.dma_start(
                out=triT[:, :].rearrange("p (k f) -> p k f", f=512),
                in_=triT_c.ap().rearrange("(k p) f -> p k f", p=128)))
            mkld = ctx.dcnt["dmk"]
            ev = {0: [], 1: [], 2: []}
            gctr = 0
            prevh = {}
            y2m = {}
            y2war = {0: [], 1: []}
            qcwar = {0: [], 1: []}
            smf = {2: [], 3: [], 4: [], 5: [], 6: [], 7: []}
            NJ = DC // 4

            def load_y2blk(h, tb):
                par = tb % 2
                rank = (tb * 512) // TC
                h2 = ((tb * 512) % TC) // 512
                ld = ctx.dma("Y", slsems[par], lambda e, par=par, rank=rank, h2=h2: [
                    e.dma_start(
                        out=y2b2[:, par * DC * 512 + i * (DC // 4) * 512:
                                 par * DC * 512 + (i + 1) * (DC // 4) * 512]
                        .rearrange("p (dc t) -> p dc t", t=512),
                        in_=y2outL[h2].ap()[rank * D + i * (D // 4):
                                            rank * D + (i + 1) * (D // 4), :]
                        .rearrange("(dc p) t -> p dc t", p=128)) for i in range(4)],
                    waits=y2war[par] + [("cc", cct[h2])], n=4)
                y2m[par] = [(slsems[par], ld)]

            def emit_qkv(h, tb, wrdy, vb0mark):
                nonlocal gctr
                par = tb % 2
                wqofs = 0
                wvofs = 0
                yofs = par * DC * 512
                groups = [("qk", rc) for rc in range(QKC)] + [("v", tt) for tt in range(4)]
                gend = {}
                for kind, idx in groups:
                    yield
                    b = gctr % 2
                    gctr += 1
                    for dc in range(DC):
                        if kind == "qk":
                            ctx.op("T", lambda e, b=b, idx=idx, dc=dc, wqofs=wqofs, yofs=yofs:
                                   e.matmul(
                                bank(b),
                                wqk[:, wqofs + dc * 2 * DH + idx * 128:
                                     wqofs + dc * 2 * DH + idx * 128 + 128],
                                y2b2[:, yofs + dc * 512: yofs + (dc + 1) * 512],
                                start=(dc == 0), stop=(dc == DC - 1)),
                                waits=(wrdy + y2m[par] + (ev[b] if dc == 0 else [])),
                                tick=(dc == DC - 1))
                        else:
                            ctx.op("T", lambda e, b=b, idx=idx, dc=dc, wvofs=wvofs, yofs=yofs:
                                   e.matmul(
                                bank(b, DH),
                                y2b2[:, yofs + dc * 512 + idx * 128:
                                     yofs + dc * 512 + idx * 128 + 128],
                                wv[:, wvofs + dc * DH: wvofs + (dc + 1) * DH],
                                start=(dc == 0), stop=(dc == DC - 1)),
                                waits=(wrdy + y2m[par] + (ev[b] if dc == 0 else [])),
                                tick=(dc == DC - 1))
                    tg = ctx.cnt["T"]
                    if kind == "qk":
                        rc = idx
                        bcol = (h * CC + rc) if rc < CC else (HPC * CC + h * CC + rc - CC)
                        sv = ctx.op("V", lambda e, b=b, rc=rc, bcol=bcol, tb=tb, par=par:
                                    e.tensor_scalar(
                            out=(Qc[:, par * CC * 512 + rc * 512:
                                    par * CC * 512 + (rc + 1) * 512] if rc < CC else
                                 Kh[:, (rc - CC) * SEQ + tb * 512:
                                    (rc - CC) * SEQ + tb * 512 + 512]),
                            in0=bank(b), scalar1=p_bq[:, bcol:bcol + 1], scalar2=None,
                            op0=ALU.add),
                            waits=[("eT", tg)] + (qcwar[par] if rc < CC else []))
                        ev[b] = [("eV", sv)]
                    else:
                        tt = idx
                        vv = ctx.op("V", lambda e, b=b, tt=tt, tb=tb: e.tensor_add(
                            out=Vh[:, (tb * 4 + tt) * DH:(tb * 4 + tt + 1) * DH],
                            in0=bank(b, DH), in1=bvb[:, :]),
                            waits=[("eT", tg)] + vb0mark)
                        ev[b] = [("eV", vv)]
                    gend[(kind, idx)] = ev[b]
                am[("kvrdy", tb)] = [("eV", ctx.cnt["V"])]
                y2war[tb % 2] = [("eT", ctx.cnt["T"])]

            def emit_softmax(h, tb):
                par = tb % 2
                qofs = par * CC * 512
                kvrdy = am[("kvrdy", tb)]
                nm4 = p_sm[:, 1056:1060]
                rmx = p_sm[:, 1064:1065]
                tmx = p_sm[:, 1065:1066]
                vn = 0
                nm4_war = am.get("nm4_read", [])
                mx = p_sm[:, 1072:1072 + 40]
                units = [(tt, fb) for tt in range(4) for fb in range(tb + 1)]
                nun = len(units)
                mxw = am.get("mxfree", [])
                for i in range(nun + 1):
                    if i < nun:
                        tt, fb = units[i]
                        pb = 2 + (i % 3)
                        if fb == 0:
                            yield
                        for cc2 in range(CC):
                            ctx.op("T", lambda e, tt=tt, fb=fb, cc2=cc2, pb=pb, qofs=qofs:
                                   e.matmul(
                                bank(pb),
                                Qc[:, qofs + cc2 * 512 + tt * 128:
                                   qofs + cc2 * 512 + tt * 128 + 128],
                                Kh[:, cc2 * SEQ + fb * 512: cc2 * SEQ + fb * 512 + 512],
                                start=(cc2 == 0), stop=(cc2 == CC - 1)),
                                waits=(kvrdy + [("dmk", mkld)] +
                                       (smf[pb] if cc2 == 0 else [])),
                                tick=(cc2 == CC - 1))
                        am[("smm", i)] = ctx.cnt["T"]
                    if i >= 1:
                        tt, fb = units[i - 1]
                        pb = 2 + ((i - 1) % 3)
                        tS_ = am[("smm", i - 1)]
                        if fb == tb:
                            va = ctx.op("V", lambda e, tt=tt, pb=pb: e.tensor_add(
                                out=stmp[:, (pb - 2) * 512:(pb - 1) * 512], in0=bank(pb),
                                in1=triS[:, tt * 512:(tt + 1) * 512]),
                                waits=[("eT", tS_)] + am.get(("stfree", pb), []))
                            am[("stfree", pb)] = [("eV", va)]
                            vm = ctx.op("V", lambda e, pb=pb, i=i: e.tensor_reduce(
                                out=mx[:, i - 1:i], in_=stmp[:, (pb - 2) * 512:(pb - 1) * 512],
                                axis=AX.X, op=ALU.max),
                                waits=[("eV", va)] + mxw)
                        else:
                            vm = ctx.op("V", lambda e, pb=pb, i=i: e.tensor_reduce(
                                out=mx[:, i - 1:i], in_=bank(pb), axis=AX.X, op=ALU.max),
                                waits=[("eT", tS_)] + mxw)
                        smf[pb] = [("eV", vm)]
                vlast = ctx.cnt["V"]
                for tt in range(4):
                    cols = [i for i, (t2_, _) in enumerate(units) if t2_ == tt]
                    a0, a1 = cols[0], cols[-1] + 1
                    vn = ctx.op("V", lambda e, tt=tt, a0=a0, a1=a1: e.tensor_reduce(
                        out=nm4[:, tt:tt + 1], in_=mx[:, a0:a1], axis=AX.X, op=ALU.max,
                        negate=True), waits=[("eV", vlast)] + nm4_war)
                am["mxfree"] = [("eV", ctx.cnt["V"])]
                yield
                tfl = 0
                for tt in range(4):
                    tfl = ctx.op("T", lambda e, tt=tt: e.transpose(
                        bank(5)[0:1, tt * 128:(tt + 1) * 128], nm4[:, tt:tt + 1],
                        p_ident[:, :]),
                        waits=[("eV", vn)] + (smf[5] if tt == 0 else []))
                am["nm4_read"] = [("eT", tfl)]
                vfl = ctx.op("V", lambda e: e.tensor_copy(
                    out=p_sm[0:1, 0:512], in_=bank(5)[0:1, 0:512]), waits=[("eT", tfl)])
                tbc = ctx.op("T", lambda e: e.matmul(
                    bank(5), p_ones_r[:, :], p_sm[0:1, 0:512], start=True, stop=True),
                    waits=[("eV", vfl)])
                vbc = ctx.op("V", lambda e: e.tensor_copy(out=nmb2[:, :], in_=bank(5)),
                             waits=[("eT", tbc)])
                nfc = 4 * (tb + 1)

                def st_mm(fc):
                    pb = 2 + (fc % 3)
                    for cc2 in range(CC):
                        ctx.op("T", lambda e, fc=fc, cc2=cc2, pb=pb, qofs=qofs: e.matmul(
                            bank(pb),
                            Kh[:, cc2 * SEQ + fc * 128: cc2 * SEQ + fc * 128 + 128],
                            Qc[:, qofs + cc2 * 512: qofs + (cc2 + 1) * 512],
                            start=(cc2 == 0), stop=(cc2 == CC - 1)),
                            waits=(smf[pb] if cc2 == 0 else []),
                            tick=(cc2 == CC - 1))
                    am[("stmm", fc)] = ctx.cnt["T"]

                def st_tail(fc):
                    pb = 2 + (fc % 3)
                    tST = am[("stmm", fc)]
                    va = ctx.op("V", lambda e, pb=pb: e.tensor_add(
                        out=stmp[:, (pb - 2) * 512:(pb - 1) * 512], in0=bank(pb),
                        in1=nmb2[:, :]),
                        waits=[("eT", tST), ("eV", vbc)] + am.get(("stfree", pb), []))
                    smf[pb] = [("eV", va)]
                    if fc >= tb * 4:
                        kk = fc - tb * 4
                        va = ctx.op("V", lambda e, pb=pb, kk=kk: e.tensor_add(
                            out=stmp[:, (pb - 2) * 512:(pb - 1) * 512],
                            in0=stmp[:, (pb - 2) * 512:(pb - 1) * 512],
                            in1=triT[:, kk * 512:(kk + 1) * 512]), waits=[("eV", va)])
                    sp = ctx.op("S", lambda e, fc=fc, pb=pb: e.activation(
                        out=ptb[:, (fc % 4) * 512:((fc % 4) + 1) * 512],
                        in_=stmp[:, (pb - 2) * 512:(pb - 1) * 512], func=AF.Exp),
                        waits=[("eV", va)] + am.get(("ptfree", fc % 4), []))
                    am[("stfree", pb)] = [("eS", sp)]
                    ctx.op("T", lambda e, fc=fc, nfc=nfc: e.matmul(
                        bank(5)[0:1, 0:512], p_ones_cb[:, :],
                        ptb[:, (fc % 4) * 512:((fc % 4) + 1) * 512],
                        start=(fc == 0), stop=(fc == nfc - 1)),
                        waits=[("eS", sp)] + ([("eV", vb0)] if fc == 0 else []),
                        tick=(fc == nfc - 1))
                    for cc2 in range(CC):
                        ctx.op("T", lambda e, fc=fc, cc2=cc2, nfc=nfc: e.matmul(
                            bank(6 + cc2),
                            Vh[:, fc * DH + cc2 * 128: fc * DH + cc2 * 128 + 128],
                            ptb[:, (fc % 4) * 512:((fc % 4) + 1) * 512],
                            start=(fc == 0), stop=(fc == nfc - 1)),
                            waits=([("eS", sp)] + (smf[6 + cc2] if fc == 0 else [])),
                            tick=(fc == nfc - 1))
                        if fc == nfc - 1:
                            am[("attdone", cc2)] = [("eT", ctx.cnt["T"])]
                    am[("ptfree", fc % 4)] = [("eT", ctx.cnt["T"])]

                for fc in range(nfc + 1):
                    if fc < nfc:
                        if fc % 2 == 0:
                            yield
                        st_mm(fc)
                    if fc >= 1:
                        st_tail(fc - 1)
                yield
                tsum = ctx.cnt["T"]
                vrc = ctx.op("V", lambda e: e.reciprocal(
                    out=p_sm[0:1, 512:1024], in_=bank(5)[0:1, 0:512]),
                    waits=[("eT", tsum)])
                smf[5] = [("eV", vrc)]
                trb = ctx.op("T", lambda e: e.matmul(
                    bank(2), p_ones_r[:, :], p_sm[0:1, 512:1024], start=True, stop=True),
                    waits=[("eV", vrc)] + smf[2])
                vrb = ctx.op("V", lambda e: e.tensor_copy(out=rcb[:, :], in_=bank(2)),
                             waits=[("eT", trb)])
                smf[2] = [("eV", vrb)]
                for cc2 in range(CC):
                    vf = ctx.op("V", lambda e, cc2=cc2: e.tensor_mul(
                        out=vo[:, cc2 * 512:(cc2 + 1) * 512], in0=bank(6 + cc2),
                        in1=rcb[:, :]),
                        waits=([("eV", vrb)] + am[("attdone", cc2)] +
                               am.get(("vofree", cc2), [])))
                    smf[6 + cc2] = [("eV", vf)]
                    dv = ctx.dma("Y", "dvs%d" % cc2, lambda e, h=h, cc2=cc2, tb=tb:
                                 e.dma_start(
                        out=vinL[h].ap()[cc2 * 128:(cc2 + 1) * 128,
                                         tb * 512:(tb + 1) * 512],
                        in_=vo[:, cc2 * 512:(cc2 + 1) * 512]), waits=[("eV", vf)])
                    am[("vofree", cc2)] = [("dvs%d" % cc2, dv)]
                qcwar[par] = [("eT", ctx.cnt["T"])]
                y2war[par] = y2war[par]

            for h in range(HPC):
                hb = h % 2
                wld = ctx.dma("G", "dwh0", lambda e, h=h: e.dma_start(
                    out=wqk[:, :],
                    in_=wqks.ap()[:, h * DC * 2 * DH:(h + 1) * DC * 2 * DH]),
                    waits=ph_gate + prevh.get("wqkfree", []))
                wld = ctx.dma("G", "dwh0", lambda e, h=h: e.dma_start(
                    out=wv[:, :],
                    in_=wvs.ap()[:, h * DC * DH:(h + 1) * DC * DH]))
                wrdy = [("dwh0", wld)]
                tb0 = ctx.op("T", lambda e, h=h: e.matmul(
                    bank(5, DH), p_ones_r[:, :],
                    p_bqf[0:1, 2 * HPC * DH + h * DH: 2 * HPC * DH + h * DH + DH],
                    start=True, stop=True),
                    waits=[("dsm", sm_all), ("dsg", sg_all), ("dmk", ctx.dcnt["dmk"])] +
                    smf[5] + ph_gate)
                vb0 = ctx.op("V", lambda e: e.tensor_copy(out=bvb[:, :], in_=bank(5, DH)),
                             waits=[("eT", tb0)] + prevh.get("bvbfree", []))
                smf[5] = [("eV", vb0)]
                vb0mark = [("eV", vb0)]
                load_y2blk(h, 0)
                for u in range(NTB + 1):
                    qk_gen = emit_qkv(h, u, wrdy, vb0mark) if u < NTB else None
                    sm_gen = emit_softmax(h, u - 1) if u >= 1 else None
                    alive_q, alive_s = qk_gen is not None, sm_gen is not None
                    while alive_q or alive_s:
                        if alive_q:
                            try:
                                next(qk_gen)
                            except StopIteration:
                                alive_q = False
                                if u + 1 < NTB:
                                    load_y2blk(h, u + 1)
                        if alive_s:
                            try:
                                next(sm_gen)
                            except StopIteration:
                                alive_s = False
                prevh["wqkfree"] = [("eT", ctx.cnt["T"])]
                prevh["bvbfree"] = [("eV", ctx.cnt["V"])]
                ctx.cc(vinL[h].ap().opt(), voutL[h].ap().opt(), groups,
                       waits=[("dvs%d" % i, ctx.dcnt["dvs%d" % i]) for i in range(CC)])
                ccvt.append(ctx.dcnt["cc"])
                ctx.flush(blk)
            ctx.flush(blk)
        bT, bV, bS = ctx.cnt["T"], ctx.cnt["V"], ctx.cnt["S"]
        ph2 = [("eT", bT), ("eV", bV), ("eS", bS)]

        with nc.sbuf_tensor("residB", [128, DC * TC], F32) as residB:
            rl = ctx.dma("Y", "drl", lambda e: [e.dma_start(
                out=residB[:, i * (DC // 4) * TC:(i + 1) * (DC // 4) * TC],
                in_=spill.ap()[:, i * (DC // 4) * TC:(i + 1) * (DC // 4) * TC])
                for i in range(4)], waits=ph2 + [("dsp", spdone)], n=4)
            opm = {}
            ctx.barrier()
            with ExitStack() as ces:
                valsr = ces.enter_context(nc.sbuf_tensor("c_vals", [128, NCCH * NTH * 512], BF16))
                wo = ces.enter_context(nc.sbuf_tensor("c_wo", [128, 2 * NCCH * 128], BF16))
                t4 = ces.enter_context(nc.sbuf_tensor("c_t4", [128, 2 * 512], F32))
                cp0 = ces.enter_context(nc.psum_tensor("c_pm0", [128, 512], F32))
                cp1 = ces.enter_context(nc.psum_tensor("c_pm1", [128, 512], F32))
                cp = [cp0, cp1]
                NCH = 4 * DH // 128
                vlt = {}
                for h in range(HPC):
                    def vload(e, h=h):
                        pid = e.partition_id()
                        off = e.snap((pid % 4) * TC, min_val=0, max_val=3 * TC)
                        return [e.dma_start(
                            out=valsr[:, (h * NCH + c) * NTH * 512:
                                      (h * NCH + c + 1) * NTH * 512],
                            in_=voutL[h].ap()[c * 128:(c + 1) * 128, :][:, bass.ds(off, TC)])
                            for c in range(NCH)]
                    sem = "dvl" if h == 0 else "dvl1"
                    ctx.dma("Y", sem, vload, waits=ph2 + [("cc", ccvt[h])], n=NCH)
                    vlt[h] = (sem, ctx.dcnt[sem])
                wofree = {}
                cfree = {}
                t4free = {}
                pend = []

                def op_epi(oc, th, tmm):
                    par = (oc * NTH + th) % 2
                    pb = cp[par]
                    s1 = ctx.op("V", lambda e, oc=oc, par=par, pb=pb: e.tensor_scalar(
                        out=t4[:, par * 512:(par + 1) * 512], in0=pb[:, 0:512],
                        scalar1=p_bo[:, oc:oc + 1], scalar2=None, op0=ALU.add),
                        waits=[("eT", tmm)] + t4free.get(par, []))
                    cfree[par] = [("eV", s1)]
                    v1 = ctx.op("V", lambda e, oc=oc, th=th, par=par: e.tensor_add(
                        out=residB[:, oc * TC + th * 512: oc * TC + th * 512 + 512],
                        in0=residB[:, oc * TC + th * 512: oc * TC + th * 512 + 512],
                        in1=t4[:, par * 512:(par + 1) * 512]),
                        waits=[("eV", s1), ("drl", rl)])
                    t4free[par] = [("eV", v1)]
                    for qq in (2 * th, 2 * th + 1):
                        opm[(oc, qq)] = [("eV", v1)]

                for oc in range(DC):
                    slw = oc % 2
                    wofs = slw * NCCH * 128
                    ld = ctx.dma("G", wosems[slw], lambda e, oc=oc, wofs=wofs: e.dma_start(
                        out=wo[:, wofs:wofs + NCCH * 128],
                        in_=wos.ap()[:, oc * NCCH * 128:(oc + 1) * NCCH * 128]),
                        waits=ph2 + wofree.get(slw, []))
                    wrdy = [(wosems[slw], ld)]
                    for th in range(NTH):
                        par = (oc * NTH + th) % 2
                        pb = cp[par]
                        for cc2 in range(NCCH):
                            ctx.op("T", lambda e, oc=oc, th=th, cc2=cc2, pb=pb, wofs=wofs:
                                   e.matmul(
                                pb[:, 0:512],
                                wo[:, wofs + cc2 * 128: wofs + cc2 * 128 + 128],
                                valsr[:, (cc2 * NTH + th) * 512:(cc2 * NTH + th + 1) * 512],
                                start=(cc2 == 0), stop=(cc2 == NCCH - 1)),
                                waits=(wrdy + [vlt[cc2 // (NCCH // HPC)]] +
                                       (cfree.get(par, []) if cc2 == 0 else [])),
                                tick=(cc2 == NCCH - 1))
                        tmm = ctx.cnt["T"]
                        if th == NTH - 1:
                            wofree[slw] = [("eT", tmm)]
                        pend.append((oc, th, tmm))
                        if len(pend) > 1:
                            op_epi(*pend.pop(0))
                for it in pend:
                    op_epi(*it)
                ctx.flush(blk)
            if stages >= 3:
                ffn_phase(1, residB,
                          {k: v for k, v in opm.items()
                           if isinstance(k, tuple) and isinstance(k[0], int)},
                          winit={i: [("dws%d" % i, ctx.dcnt["dws%d" % i])]
                                 for i in range(3)})
            ctx.dma("Y", "dout", lambda e: [e.dma_start(
                out=out_t.ap()[i * (D // 4):(i + 1) * (D // 4), :]
                .rearrange("(dc p) t -> p dc t", p=128),
                in_=residB[:, i * (DC // 4) * TC:(i + 1) * (DC // 4) * TC]
                .rearrange("p (dc t) -> p dc t", t=TC))
                for i in range(4)],
                waits=[("eV", ctx.cnt["V"]), ("eS", ctx.cnt["S"])], n=4)
            ctx.op("G", lambda e: e.nop(), waits=[("dout", ctx.dcnt["dout"])], tick=False)
            ctx.op("Y", lambda e: e.nop(), waits=[("dout", ctx.dcnt["dout"])], tick=False)
            ctx.flush(blk)

    return nc


def shard_inputs(cfg, L0_attn_x, L0_attn_w_qkv, L0_attn_b_qkv, L0_attn_w_o,
                 L0_attn_b_o, L0_ffn_weight, L0_ffn_bias, L0_ffn_gate_w,
                 L0_ffn_in_channel_scale, L0_ffn_out_channel_scale):
    D, TC, DH, HPC = cfg["D"], cfg["TC"], cfg["DH"], cfg["HPC"]
    PART = 4 * HPC * DH
    f32 = np.float32
    x = np.asarray(L0_attn_x, f32)
    wq = np.asarray(L0_attn_w_qkv, f32)
    bq = np.asarray(L0_attn_b_qkv, f32)
    wo = np.asarray(L0_attn_w_o, f32)
    bo = np.asarray(L0_attn_b_o, f32)
    wf = np.asarray(L0_ffn_weight, f32)
    bf = np.asarray(L0_ffn_bias, f32)
    gw = np.asarray(L0_ffn_gate_w, f32)
    icsv = np.asarray(L0_ffn_in_channel_scale, f32)
    ocsv = np.asarray(L0_ffn_out_channel_scale, f32)

    DC = D // 128
    NCCH = PART // 128
    NR = 4
    HPC_ = HPC
    perm = []
    for h in range(HPC_):
        for r in range(NR):
            base = r * HPC_ * DH + h * DH
            perm.extend(range(base, base + DH))
    woT = np.ascontiguousarray(wo.T[perm, :])
    wfT = np.ascontiguousarray(wf.T)
    wfs = np.ascontiguousarray(
        wfT.reshape(DC, 128, DC, 128).transpose(1, 2, 0, 3).reshape(128, DC * DC * 128))
    wos_a = np.ascontiguousarray(
        woT.reshape(NCCH, 128, DC, 128).transpose(1, 2, 0, 3).reshape(128, DC * NCCH * 128))
    com = dict(
        wfs=wfs, wos=wos_a,
        w_oT=np.ascontiguousarray(wo.T),
        b_o=np.ascontiguousarray(bo[None, :]),
        bo_pp=np.ascontiguousarray(bo.reshape(-1, 128).T),
        w_ffnT=np.ascontiguousarray(wf.T),
        b_ffn=np.ascontiguousarray(bf[None, :]),
        bf_pp=np.ascontiguousarray(bf.reshape(-1, 128).T),
        gateT=np.ascontiguousarray(gw.T),
        ics=np.ascontiguousarray(icsv),
        ocs=np.ascontiguousarray(ocsv),
    )
    in_maps = []
    for c in range(8):
        bidx, r = c // 4, c % 4
        xs = np.ascontiguousarray(x[bidx, r * TC:(r + 1) * TC, :].T)
        hs = [HPC * r + i for i in range(HPC)]
        qcols = np.concatenate([wq[h * DH:(h + 1) * DH, :].T for h in hs], axis=1)
        kcols = np.concatenate([wq[PART + h * DH: PART + (h + 1) * DH, :].T for h in hs], axis=1)
        vcols = np.concatenate([wq[2 * PART + h * DH: 2 * PART + (h + 1) * DH, :].T for h in hs], axis=1)
        wqkvT = np.ascontiguousarray(np.concatenate([qcols, kcols, vcols], axis=1))
        bq_sel = np.concatenate(
            [np.concatenate([bq[h * DH:(h + 1) * DH] for h in hs])] +
            [np.concatenate([bq[PART + h * DH: PART + (h + 1) * DH] for h in hs])] +
            [np.concatenate([bq[2 * PART + h * DH: 2 * PART + (h + 1) * DH] for h in hs])])
        qT = qcols.reshape(DC, 128, HPC, DH).transpose(1, 2, 0, 3)
        kT = kcols.reshape(DC, 128, HPC, DH).transpose(1, 2, 0, 3)
        vT = vcols.reshape(DC, 128, HPC, DH).transpose(1, 2, 0, 3)
        wqks = np.ascontiguousarray(
            np.concatenate([qT, kT], axis=3).reshape(128, HPC * DC * 2 * DH))
        wvs = np.ascontiguousarray(vT.reshape(128, HPC * DC * DH))
        m = dict(x_t=xs, wqkvT=wqkvT, bqkv=np.ascontiguousarray(bq_sel[None, :]),
                 bqkv_pp=np.ascontiguousarray(bq_sel.reshape(-1, 128).T),
                 wqks=wqks, wvs=wvs, **com)
        in_maps.append(m)
    return in_maps


def assemble(cfg, results):
    D, TC = cfg["D"], cfg["TC"]
    out = np.empty((2, 4 * TC, D), np.float32)
    for c in range(8):
        bidx, r = c // 4, c % 4
        out[bidx, r * TC:(r + 1) * TC, :] = results[c]["out_t"].T
    return out


def kernel(**inputs):
    cfg = FULL
    nc = build(cfg)
    in_maps = shard_inputs(cfg, **inputs)
    res = run_bass_kernel_spmd(nc, in_maps, core_ids=list(range(8)))
    return assemble(cfg, res.results)
